# version 22
# baseline (speedup 1.0000x reference)
import contextlib

import numpy as np
import ml_dtypes

import concourse.bass as bass
import concourse.mybir as mybir
from concourse.bass_utils import run_bass_kernel_spmd

F32 = mybir.dt.float32
BF16 = mybir.dt.bfloat16
AF = mybir.ActivationFunctionType
ALU = mybir.AluOpType
bfloat16 = ml_dtypes.bfloat16

D = 1024
H = 16
HD = 64
ROPE = 32
LAT = 128
NCH = 8
QT = 512
KB = 128
N_CORES = 8

(CB_EXPB, CB_EPSR, CB_EPSL, CB_ZERO, CB_ONE, CB_R1024, CB_RN1024,
 CB_TINY) = range(8)
CONST_VALS = [-6.0, 1e-6, 1e-5, 0.0, 1.0, 1.0 / 1024.0, -1.0 / 1024.0,
              1e-30]

ENGINES = ("sync", "tensor", "vector", "scalar", "gpsimd")


class Op:
    __slots__ = ("eng", "fn", "deps", "inc", "count", "waits", "dma",
                 "chan", "rw")

    def __init__(self, eng, fn, deps, dma):
        self.rw = None
        self.eng = eng
        self.fn = fn
        self.deps = deps
        self.dma = dma
        self.chan = None
        self.inc = False
        self.count = None
        self.waits = []


class Prog:

    def __init__(self):
        self.ops = {e: [] for e in ENGINES}
        self.all = []
        self.bufs = {}

    def _pre(self, reads, writes):
        deps = []
        for k in reads:
            st = self.bufs.get(k)
            if st and st["w"] is not None:
                deps.append(st["w"])
        for k in writes:
            st = self.bufs.get(k)
            if st:
                if st["r"]:
                    deps.extend(st["r"])
                elif st["w"] is not None:
                    deps.append(st["w"])
        return deps

    def _post(self, op, reads, writes):
        for k in reads:
            self.bufs.setdefault(k, {"w": None, "r": []})["r"].append(op)
        for k in writes:
            self.bufs[k] = {"w": op, "r": []}

    def op(self, eng, fn, reads=(), writes=(), deps=(), dma=None):
        d = self._pre(reads, writes) + [x for x in deps if x is not None]
        o = Op(eng, fn, d, dma)
        o.rw = (tuple(reads), tuple(writes))
        self.ops[eng].append(o)
        self.all.append(o)
        self._post(o, reads, writes)
        return o

    def group(self, eng, fns, reads=(), writes=(), deps0=(), mreads=None):
        d0 = self._pre(reads, writes) + [x for x in deps0 if x is not None]
        last = None
        regs = list(reads)
        for i, (fn, extra) in enumerate(fns):
            dd = (d0 if i == 0 else []) + [x for x in extra if x is not None]
            mr = ()
            if mreads is not None:
                mr = tuple(mreads[i])
                regs.extend(mr)
                for k in mr:
                    st = self.bufs.get(k)
                    if st and st["w"] is not None:
                        dd.append(st["w"])
            if last is not None:
                dd = dd + [last]
            nxt = Op(eng, fn, dd, False)
            nxt.rw = (tuple(reads) + mr, tuple(writes))
            self.ops[eng].append(nxt)
            self.all.append(nxt)
            last = nxt
        self._post(last, tuple(regs), writes)
        return last

    def wbegin(self, keys):
        return self._pre((), keys)

    def wend(self, op, keys):
        for k in keys:
            self.bufs[k] = {"w": op, "r": []}

    def est_cost(self, o):
        if o.dma:
            return 790.0
        if o.eng == "tensor":
            rw = o.rw[0] + o.rw[1]
            if any(k.startswith(("eb", "vtm", "ks")) for k in rw) or \
               "onf" in o.rw[0][:1]:
                return 110.0
            return 240.0
        if o.eng == "scalar":
            return 1000.0 if any(k.startswith("eb") for k in o.rw[1]) \
                else 650.0
        if o.eng == "gpsimd":
            return 1300.0
        return 620.0

    def schedule(self):
        import heapq
        PROP = 200.0
        DMA_LAT = 600.0
        succs = {id(o): [] for o in self.all}
        indeg = {id(o): 0 for o in self.all}
        for o in self.all:
            for d in o.deps:
                succs[id(d)].append(o)
                indeg[id(o)] += 1
        prio = {}
        for o in reversed(self.all):
            c = self.est_cost(o)
            best = 0.0
            for s in succs[id(o)]:
                best = max(best, prio[id(s)])
            prio[id(o)] = c + best

        avail = {e: 0.0 for e in ENGINES}
        done_t = {}
        ready = {e: [] for e in ENGINES}
        seq = {e: 0 for e in ENGINES}
        emit_idx = {id(o): i for i, o in enumerate(self.all)}
        for o in self.all:
            if indeg[id(o)] == 0:
                heapq.heappush(ready[o.eng],
                               (-prio[id(o)], emit_idx[id(o)], id(o), o))
        new_ops = {e: [] for e in ENGINES}
        n_done = 0
        pending = []
        def start_time(o, e):
            t0 = avail[e]
            for d in o.deps:
                t0 = max(t0, done_t[id(d)] +
                         (0.0 if d.eng == o.eng and not d.dma else PROP))
            return t0
        while n_done < len(self.all):
            best = None
            for e in ENGINES:
                if not ready[e]:
                    continue
                cand = heapq.nsmallest(8, ready[e])
                eb = None
                for item in cand:
                    o = item[3]
                    t0 = start_time(o, e)
                    if eb is None or t0 < eb[0] - 200.0 or \
                       (abs(t0 - eb[0]) <= 200.0 and
                        prio[id(o)] > prio[id(eb[1])]):
                        eb = (t0, o, item)
                if best is None or eb[0] < best[0] or \
                   (eb[0] == best[0] and prio[id(eb[1])] > prio[id(best[1])]):
                    best = (eb[0], eb[1], e, eb[2])
            t0, o, e, item = best
            ready[e].remove(item)
            heapq.heapify(ready[e])
            c = self.est_cost(o)
            if o.dma:
                avail[e] = t0 + c
                fin = t0 + c + DMA_LAT
            else:
                avail[e] = t0 + c
                fin = t0 + c
            done_t[id(o)] = fin
            new_ops[e].append(o)
            n_done += 1
            for s in succs[id(o)]:
                indeg[id(s)] -= 1
                if indeg[id(s)] == 0:
                    heapq.heappush(ready[s.eng],
                                   (-prio[id(s)], emit_idx[id(s)], id(s), s))
        self.ops = new_ops

    def finalize(self):
        for o in self.all:
            for d in o.deps:
                if d.eng != o.eng or d.dma:
                    d.inc = True
        cnt = {}
        for e in ENGINES:
            for o in self.ops[e]:
                o.chan = o.dma if o.dma else e
                if o.dma:
                    o.inc = True
                if o.inc:
                    cnt[o.chan] = cnt.get(o.chan, 0) + (16 if o.dma else 1)
                    o.count = cnt[o.chan]
        hwm = {e: {} for e in ENGINES}
        for e in ENGINES:
            for o in self.ops[e]:
                for d in o.deps:
                    if d.chan == e and not d.dma:
                        continue
                    if d.count > hwm[e].get(d.chan, 0):
                        o.waits.append((d.chan, d.count))
                        hwm[e][d.chan] = d.count


def drive(gf, gt, ga, nf=1, nt=1, na=1):
    rem = {}
    for g, n in ((gf, nf), (gt, nt), (ga, na)):
        if g is not None:
            rem[g] = max(1, n)
    t_blocked = False
    while rem:
        order = sorted(rem, key=lambda g: -rem[g])
        g = order[0]
        if g is gt and t_blocked:
            others = [x for x in order[1:]]
            if gf in rem:
                g = others[0] if others else None
                if g is None:
                    continue
            else:
                t_blocked = False
        try:
            v = next(g)
            rem[g] = max(1, rem[g] - 1)
            if g is gt and v == "wo" and gf in rem:
                t_blocked = True
        except StopIteration:
            del rem[g]


def build_program(n_quarters=4, b_core=2):
    T = n_quarters * QT
    NKC = T // KB

    nc = bass.Bass("TRN2", target_bir_lowering=False,
                   dynamic_dma_scratch_size=512,
                   detect_race_conditions=False)

    x0_d = nc.dram_tensor("x0", [b_core, n_quarters, D, QT], F32,
                          kind="ExternalInput").ap()
    out_d = nc.dram_tensor("out", [b_core, n_quarters, D, QT], F32,
                           kind="ExternalOutput").ap()
    w_d = {}
    for nm, shp in [("wq", [D, D]), ("wkv", [D, LAT]),
                    ("wk", [LAT, H * ROPE]), ("wv", [LAT, H * ROPE]),
                    ("wo", [H * ROPE, D]), ("wrt", [D, D]),
                    ("wkt", [D, D]), ("wvt", [D, D])]:
        w_d[nm] = nc.dram_tensor(nm, shp, F32, kind="ExternalInput").ap()
    mur_d = nc.dram_tensor("mur", [128, NCH], F32, kind="ExternalInput").ap()
    muk_d = nc.dram_tensor("muk", [128, NCH], F32, kind="ExternalInput").ap()
    qg_d = nc.dram_tensor("qg", [128, 1], F32, kind="ExternalInput").ap()
    kg_d = nc.dram_tensor("kg", [128, 1], F32, kind="ExternalInput").ap()
    cst_d = nc.dram_tensor("cst", [128, len(CONST_VALS)], F32,
                           kind="ExternalInput").ap()
    tri_d = nc.dram_tensor("tri", [128, 256], BF16, kind="ExternalInput").ap()
    obd_d = nc.dram_tensor("obd", [128, 128], BF16, kind="ExternalInput").ap()
    onf_d = nc.dram_tensor("onf", [128, 128], BF16, kind="ExternalInput").ap()

    with contextlib.ExitStack() as es:
        EC = es.enter_context

        wq = EC(nc.sbuf_tensor("wqs", [128, NCH * D], BF16))
        wkv = EC(nc.sbuf_tensor("wkvs", [128, NCH * LAT], BF16))
        wk = EC(nc.sbuf_tensor("wks", [128, H * ROPE], BF16))
        wv = EC(nc.sbuf_tensor("wvs", [128, H * ROPE], BF16))
        wo = EC(nc.sbuf_tensor("wos", [128, 4 * D], BF16))
        wrt = EC(nc.sbuf_tensor("wrts", [128, NCH * D], BF16))
        wkt = EC(nc.sbuf_tensor("wkts", [128, NCH * D], BF16))
        wvt = EC(nc.sbuf_tensor("wvts", [128, NCH * D], BF16))
        mur = EC(nc.sbuf_tensor("murs", [128, NCH], F32))
        muk = EC(nc.sbuf_tensor("muks", [128, NCH], F32))
        qg = EC(nc.sbuf_tensor("qgs", [128, 1], F32))
        kg = EC(nc.sbuf_tensor("kgs", [128, 1], F32))
        cst = EC(nc.sbuf_tensor("csts", [128, len(CONST_VALS)], F32))
        tri = EC(nc.sbuf_tensor("tris", [128, 256], BF16))
        obd = EC(nc.sbuf_tensor("obds", [128, 128], BF16))
        onf = EC(nc.sbuf_tensor("onfs", [128, 128], BF16))
        ybuf = EC(nc.sbuf_tensor("ybuf", [128, NCH * (QT + 1)], F32))
        x0bf = EC(nc.sbuf_tensor("x0bf", [128, NCH * QT], BF16))
        xr = EC(nc.sbuf_tensor("xrb", [128, NCH * QT], BF16))
        xk = EC(nc.sbuf_tensor("xkb", [128, NCH * QT], BF16))
        qs2 = EC(nc.sbuf_tensor("qs2", [128, 2 * 4 * QT], BF16))
        ks = EC(nc.sbuf_tensor("ksb", [128, 4 * T], BF16))
        kvbf = EC(nc.sbuf_tensor("kvbf", [128, QT], BF16))
        ebuf = EC(nc.sbuf_tensor("ebuf", [128, 2 * 2048], BF16))
        outc = EC(nc.sbuf_tensor("outcb", [128, 4 * QT], BF16))
        esig = EC(nc.sbuf_tensor("esig", [128, NCH * QT], BF16))
        kact = EC(nc.sbuf_tensor("kactb", [128, NCH * QT], BF16))
        x0r = EC(nc.sbuf_tensor("x0r", [128, 2 * 512], F32))
        x1r = EC(nc.sbuf_tensor("x1r", [128, 2 * 512], F32))
        scA1 = EC(nc.sbuf_tensor("scA1", [128, 512], F32))
        scT0 = EC(nc.sbuf_tensor("scT0", [128, 512], F32))
        scT1 = EC(nc.sbuf_tensor("scT1", [128, 512], F32))
        scF0 = EC(nc.sbuf_tensor("scF0", [128, 512], F32))
        scF1 = EC(nc.sbuf_tensor("scF1", [128, 512], F32))
        bfA0 = EC(nc.sbuf_tensor("bfA0", [128, 512], BF16))
        bfA1 = EC(nc.sbuf_tensor("bfA1", [128, 512], BF16))
        bfA2 = EC(nc.sbuf_tensor("bfA2", [128, 512], BF16))
        bfT0 = EC(nc.sbuf_tensor("bfT0", [128, 512], BF16))
        bfT1 = EC(nc.sbuf_tensor("bfT1", [128, 512], BF16))
        bfF0 = EC(nc.sbuf_tensor("bfF0", [128, 512], BF16))
        bfF1 = EC(nc.sbuf_tensor("bfF1", [128, 512], BF16))
        bfF2 = EC(nc.sbuf_tensor("bfF2", [128, 512], BF16))
        bfT2 = EC(nc.sbuf_tensor("bfT2", [128, 512], BF16))
        vtm = EC(nc.sbuf_tensor("vtm", [128, NKC * 768], BF16))

        sp = EC(nc.psum_tensor("spp", [128, 2048], F32))
        pvp = EC(nc.psum_tensor("pvp", [128, 512], F32))
        nmp = EC(nc.psum_tensor("nmp", [128, 512], F32))
        gap = EC(nc.psum_tensor("gap", [128, 512], F32))
        gbp = EC(nc.psum_tensor("gbp", [128, 512], F32))

        DMA_CHANS = ("dma_c", "dma_w0", "dma_w1", "x0r0", "x0r1",
                     "x1r0", "x1r1", "or0", "or1")
        sems = {e: EC(nc.semaphore("sem_" + e))
                for e in list(ENGINES) + list(DMA_CHANS)}
        P = Prog()

        CZ = cst[:, CB_ZERO:CB_ZERO + 1]

        def load(dst_ap, src_ap, wkey, chan):
            return P.op("sync",
                        lambda s, dd=dst_ap, rr=src_ap: s.dma_start(dd, rr),
                        writes=(wkey,), dma=chan)

        load(cst[:, :], cst_d, "cst", "dma_c")
        load(mur[:, :], mur_d, "mur", "dma_c")
        load(muk[:, :], muk_d, "muk", "dma_c")
        load(qg[:, :], qg_d, "qg", "dma_c")
        load(kg[:, :], kg_d, "kg", "dma_c")
        load(tri[:, :], tri_d, "tri", "dma_c")
        load(obd[:, :], obd_d, "obd", "dma_c")
        load(onf[:, :], onf_d, "onf", "dma_c")

        wslot = [0]

        def wload(name, n_chunks, cols, dst, dst_cols):
            for ch in range(n_chunks):
                prev = None
                for p0 in range(0, cols, 256):
                    pw = min(256, cols - p0)
                    sl = wslot[0] % 2
                    wslot[0] += 1
                    src = w_d[name][ch * 128:(ch + 1) * 128, p0:p0 + pw]
                    P.op("sync",
                         lambda s, sr=src, w=pw, s_=sl: s.dma_start(
                             scF1[:, s_ * 256:s_ * 256 + w], sr),
                         writes=(f"stage{sl}",), dma=f"dma_w{sl}")
                    last = p0 + pw >= cols
                    prev = P.op(
                        "vector",
                        lambda v, c=ch, w=pw, q0=p0, dd=dst, dc=dst_cols,
                        s_=sl:
                        v.tensor_copy(dd[:, c * dc + q0:c * dc + q0 + w],
                                      scF1[:, s_ * 256:s_ * 256 + w]),
                        reads=(f"stage{sl}",),
                        writes=(f"{name}{ch}",) if last else (),
                        deps=(prev,))

        wload("wq", NCH, D, wq, D)
        _stage_fence_pending = True
        wload("wkv", NCH, LAT, wkv, LAT)
        wload("wk", 1, H * ROPE, wk, H * ROPE)
        wload("wv", 1, H * ROPE, wv, H * ROPE)
        wload("wo", 4, D, wo, D)
        wload("wrt", NCH, D, wrt, D)
        wload("wkt", NCH, D, wkt, D)
        wload("wvt", NCH, D, wvt, D)
        P.op("vector", lambda v: v.memset(scF1[0:1, 0:1], 0.0),
             writes=("stage0", "stage1", "scF1"))

        P.op("vector", lambda v: v.tensor_copy(
            bass.AP(vtm.ap().tensor, 32,
                    [[NKC * 768, 128], [96, NKC * 8], [1, 32]]),
            bass.AP(onf.ap().tensor, 0,
                    [[128, 128], [0, NKC * 8], [1, 32]])),
            reads=("onf",), writes=("vones",))

        for cc in range(NCH):
            P.op("vector", lambda v, c=cc: v.memset(
                ybuf[:, c * (QT + 1):c * (QT + 1) + 1], 0.0),
                writes=(f"yb{cc}",))

        store_ops = []


        def stA(b, Q):
            par = Q % 2
            qb0 = par * 4 * QT
            qkey = f"qs{par}"
            cast_ops = []
            for w in range(4):
                for sl in range(2):
                    cc = 2 * w + sl
                    load(x0r[:, sl * 512:(sl + 1) * 512],
                         x0_d[b, Q][cc * 128:(cc + 1) * 128, :], f"x0r{sl}",
                         f"x0r{sl}")
                cw = P.op(
                    "vector", lambda v, w_=w: v.tensor_copy(
                        x0bf[:, w_ * 2 * QT:(w_ + 1) * 2 * QT],
                        x0r[:, :]),
                    reads=("x0r0", "x0r1"),
                    writes=tuple(f"x0bf{2 * w + s}" for s in range(2)))
                cast_ops.extend([cw] * 2)
                if w % 2 == 1:
                    yield
            wqr = tuple(f"wq{c}" for c in range(NCH))
            x0keys = tuple(f"x0bf{c}" for c in range(NCH))
            for g in range(4):
                fns = [(lambda t, c=cc_, o=4 + g, st=(cc_ == 0),
                        sp_=(cc_ == NCH - 1): t.matmul(
                            gap[:, :],
                            wq[:, c * D + o * 128:c * D + (o + 1) * 128],
                            x0bf[:, c * QT:(c + 1) * QT],
                            start=st, stop=sp_, skip_group_check=True),
                        [cast_ops[cc_]])
                       for cc_ in range(NCH)]
                P.group("tensor", fns, writes=("ga",),
                        mreads=[(f"wq{c}", f"x0bf{c}") for c in range(NCH)])
                P.op("vector", lambda v: v.tensor_copy(bfA0[:, :],
                                                       gap[:, :]),
                    reads=("ga",), writes=("bfA0",))
                P.op("vector", lambda v: v.tensor_tensor(
                    bfA2[:, :], bfA0[:, :], bfA0[:, :], op=ALU.mult),
                    reads=("bfA0",), writes=("bfA2",))
                yield
                fns = [(lambda t, c=cc_, o=g, st=(cc_ == 0),
                        sp_=(cc_ == NCH - 1): t.matmul(
                            gbp[:, :],
                            wq[:, c * D + o * 128:c * D + (o + 1) * 128],
                            x0bf[:, c * QT:(c + 1) * QT],
                            start=st, stop=sp_, skip_group_check=True),
                        [cast_ops[cc_]])
                       for cc_ in range(NCH)]
                P.group("tensor", fns, writes=("gb",),
                        mreads=[(f"wq{c}", f"x0bf{c}") for c in range(NCH)])
                P.op("vector", lambda v: v.tensor_copy(bfA1[:, :],
                                                       gbp[:, :]),
                    reads=("gb",), writes=("bfA1",))
                wdg = P.wbegin(("ga",))
                m1 = P.op("tensor", lambda t: t.matmul(
                    gap[:, :], obd[:, :], bfA2[:, :], start=True,
                    stop=False, skip_group_check=True),
                    reads=("obd", "bfA2"), deps=wdg)
                P.op("vector", lambda v: v.tensor_tensor(
                    bfA2[:, :], bfA1[:, :], bfA1[:, :], op=ALU.mult),
                    reads=("bfA1",), writes=("bfA2",))
                m2 = P.op("tensor", lambda t: t.matmul(
                    gap[:, :], obd[:, :], bfA2[:, :], start=False,
                    stop=True, skip_group_check=True),
                    reads=("obd", "bfA2"))
                P.wend(m2, ("ga",))
                P.op("vector", lambda v: v.tensor_copy(scA1[:, :],
                                                       gap[:, :]),
                    reads=("ga",), writes=("scA1",))
                P.op("scalar", lambda s: s.activation(
                    scA1[:, :], scA1[:, :], AF.Ln,
                    bias=cst[:, CB_EPSR:CB_EPSR + 1], scale=1.0 / 64.0),
                    reads=("scA1", "cst"), writes=("scA1",))
                P.op("scalar", lambda s: s.activation(
                    scA1[:, :], scA1[:, :], AF.Exp, bias=CZ, scale=-0.5),
                    reads=("scA1",), writes=("scA1",))
                P.op("vector", lambda v, g_=g: v.scalar_tensor_tensor(
                    qs2[:, qb0 + g_ * QT:qb0 + (g_ + 1) * QT], bfA1[:, :],
                    qg[:, 0:1], scA1[:, :], op0=ALU.mult, op1=ALU.mult),
                    reads=("bfA1", "qg", "scA1"), writes=(f"{qkey}_{g}",))
                yield
            fns = [(lambda t, c=cc_, st=(cc_ == 0), sp_=(cc_ == NCH - 1):
                    t.matmul(gap[:, :], wkv[:, c * LAT:(c + 1) * LAT],
                             x0bf[:, c * QT:(c + 1) * QT],
                             start=st, stop=sp_, skip_group_check=True),
                    [cast_ops[cc_]])
                   for cc_ in range(NCH)]
            P.group("tensor", fns, writes=("ga",),
                    mreads=[(f"wkv{c}", f"x0bf{c}") for c in range(NCH)])
            P.op("vector", lambda v: v.tensor_copy(kvbf[:, :], gap[:, :]),
                 reads=("ga",), writes=("kvbf",))
            yield
            for g in range(4):
                P.group("tensor",
                        [(lambda t, g_=g: t.matmul(
                            gap[:, :], wk[:, g_ * 128:(g_ + 1) * 128],
                            kvbf[:, :], start=True, stop=True,
                            skip_group_check=True), [])],
                        reads=("wk0", "kvbf"), writes=("ga",))
                P.op("vector", lambda v: v.tensor_copy(bfA1[:, :],
                                                       gap[:, :]),
                    reads=("ga",), writes=("bfA1",))
                P.op("vector", lambda v: v.tensor_tensor(
                    bfA2[:, :], bfA1[:, :], bfA1[:, :], op=ALU.mult),
                    reads=("bfA1",), writes=("bfA2",))
                P.group("tensor",
                        [(lambda t: t.matmul(
                            gbp[:, :], obd[:, :], bfA2[:, :],
                            start=True, stop=True,
                            skip_group_check=True), [])],
                        reads=("obd", "bfA2"), writes=("gb",))
                P.op("vector", lambda v: v.tensor_copy(scA1[:, :],
                                                       gbp[:, :]),
                    reads=("gb",), writes=("scA1",))
                P.op("scalar", lambda s: s.activation(
                    scA1[:, :], scA1[:, :], AF.Ln,
                    bias=cst[:, CB_EPSR:CB_EPSR + 1], scale=1.0 / 32.0),
                    reads=("scA1", "cst"), writes=("scA1",))
                P.op("scalar", lambda s: s.activation(
                    scA1[:, :], scA1[:, :], AF.Exp, bias=CZ, scale=-0.5),
                    reads=("scA1",), writes=("scA1",))
                P.op("vector", lambda v, g_=g, q_=Q: v.scalar_tensor_tensor(
                    ks[:, g_ * T + q_ * QT:g_ * T + (q_ + 1) * QT],
                    bfA1[:, :], kg[:, 0:1], scA1[:, :],
                    op0=ALU.mult, op1=ALU.mult),
                    reads=("bfA1", "kg", "scA1"), writes=(f"ks{g}q{Q}",))
                yield
                kc = 4 * Q + g
                P.group("tensor",
                        [(lambda t, j_=g: t.matmul(
                            gbp[:, :], kvbf[:, j_ * 128:(j_ + 1) * 128],
                            wv[:, :], start=True, stop=True,
                            skip_group_check=True), [])],
                        reads=("kvbf", "wv0"), writes=("gb",))
                P.op("vector", lambda v, k_=kc: v.tensor_copy(
                    bass.AP(vtm.ap().tensor, k_ * 768,
                            [[NKC * 768, 128], [96, 8], [1, 32]]),
                    bass.AP(gbp.ap().tensor, 0,
                            [[512, 128], [64, 8], [1, 32]])),
                    reads=("gb",), writes=(f"vtm{kc}",))
                P.op("vector", lambda v, k_=kc: v.tensor_copy(
                    bass.AP(vtm.ap().tensor, k_ * 768 + 64,
                            [[NKC * 768, 128], [96, 8], [1, 32]]),
                    bass.AP(gbp.ap().tensor, 32,
                            [[512, 128], [64, 8], [1, 32]])),
                    reads=("gb",), writes=(f"vtm{kc}",))
                yield

        def stT(b, Q):
            par = Q % 2
            qb0 = par * 4 * QT
            qkey = f"qs{par}"
            NKCQ = 4 * (Q + 1)
            for g in range(4):
                wdeps_pv = P.wbegin(("pv",))
                wdeps_nm = P.wbegin(("nm",))
                last_pv = last_nm = first_pv = first_nm = None
                pv_groups, nm_groups = [], []
                for kc in range(NKCQ):
                    diag = kc >= 4 * Q
                    jd = kc - 4 * Q
                    qoff = 128 * jd if diag else 0
                    nq = QT - qoff
                    slot = kc % 2
                    ekey = f"eb{slot}"
                    sc_half = []
                    prev_mm = None
                    for hh in range(4):
                        d = [prev_mm]
                        if hh == 0:
                            d += P.wbegin(("sp0", "sp1", "sp2", "sp3"))
                        prev_mm = P.op(
                            "tensor",
                            lambda t, h=hh, g_=g, k_=kc, qo=qoff, n=nq:
                            t.matmul(
                                sp[:, 512 * h + qo:512 * h + qo + n],
                                ks[32 * h:32 * (h + 1),
                                   g_ * T + k_ * KB:
                                   g_ * T + (k_ + 1) * KB],
                                qs2[32 * h:32 * (h + 1),
                                    qb0 + g_ * QT + qo:
                                    qb0 + g_ * QT + qo + n],
                                start=True, stop=True,
                                tile_position=(32 * h, 0),
                                skip_group_check=True),
                            reads=(f"ks{g}q{kc // 4}", f"{qkey}_{g}"),
                            deps=d)
                        if hh == 1:
                            P.wend(prev_mm, ("sp0", "sp1"))
                            sc_half.append(prev_mm)
                        elif hh == 3:
                            P.wend(prev_mm, ("sp2", "sp3"))
                            sc_half.append(prev_mm)
                    e_half = []
                    for half in range(2):
                        e_half.append(P.op(
                            "scalar",
                            lambda s, sl=slot, qo=qoff, n=nq, hf=half:
                            s.activation(
                                ebuf[:, sl * 2048:(sl + 1) * 2048].rearrange(
                                    "p (h q) -> p h q",
                                    h=4)[:, 2 * hf:2 * hf + 2, qo:qo + n],
                                sp[:, :].rearrange(
                                    "p (h q) -> p h q",
                                    h=4)[:, 2 * hf:2 * hf + 2, qo:qo + n],
                                AF.Exp, bias=cst[:, CB_EXPB:CB_EXPB + 1],
                                scale=0.125),
                            reads=(f"sp{2 * half}", f"sp{2 * half + 1}",
                                   "cst"),
                            writes=(f"{ekey}_{half}",)))
                    if diag:
                        m_half = []
                        for half in range(2):
                            m_half.append(P.op(
                                "vector",
                                lambda v, sl=slot, qo=qoff, hf=half:
                                v.tensor_tensor(
                                    ebuf[:, sl * 2048:(sl + 1) * 2048]
                                    .rearrange("p (h q) -> p h q", h=4)
                                    [:, 2 * hf:2 * hf + 2, qo:qo + 128],
                                    ebuf[:, sl * 2048:(sl + 1) * 2048]
                                    .rearrange("p (h q) -> p h q", h=4)
                                    [:, 2 * hf:2 * hf + 2, qo:qo + 128],
                                    tri[:, :].rearrange(
                                        "p (r q) -> p r q", r=2),
                                    op=ALU.mult),
                                reads=(f"{ekey}_{half}", "tri"),
                                writes=(f"{ekey}_{half}",)))
                        e_half = m_half
                    fns = []
                    for jj in range(4):
                        vo = (kc * 768 + 96 * (2 * g + jj // 2)
                              + 32 * (jj % 2))
                        ba_ = pvp if jj < 2 else nmp
                        fns.append((
                            lambda t, j_=jj, vo_=vo, ba=ba_, qo=qoff, n=nq,
                            sl=slot, st=(kc == 0), sp_=(kc == NKCQ - 1):
                            t.matmul(
                                ba[64 * (j_ % 2):64 * (j_ % 2) + 64,
                                   qo:qo + n],
                                vtm[:, vo_:vo_ + 64],
                                ebuf[:, sl * 2048 + 512 * j_ + qo:
                                     sl * 2048 + 512 * j_ + qo + n],
                                start=st, stop=sp_,
                                tile_position=(0, 64 * (j_ % 2)),
                                skip_group_check=True),
                            [e_half[jj // 2]]))
                    g_pv = P.group(
                        "tensor", fns,
                        reads=(f"vtm{kc}", "vones"),
                        mreads=[(f"{ekey}_{jj // 2}",) for jj in range(4)],
                        deps0=((wdeps_pv + wdeps_nm) if kc == 0
                               else [first_pv]))
                    if kc == 0:
                        first_pv = g_pv
                    pv_groups.append(g_pv)
                    last_pv = g_pv
                    yield
                P.wend(last_pv, ("pv", "nm"))
                allg = tuple(pv_groups)
                for sc_, k_, bp_, bk_ in ((scT1, "scT1", pvp, "pv"),
                                          (scT0, "scT0", nmp, "nm")):
                    P.op("scalar", lambda s, d=sc_, p_=bp_: s.activation(
                        d[:, :], p_[:, :], AF.Square, bias=CZ),
                        reads=(bk_, "cst"), writes=(k_,), deps=allg)
                    P.op("scalar", lambda s, d=sc_: s.activation(
                        d[:, :], d[:, :], AF.Ln,
                        bias=cst[:, CB_TINY:CB_TINY + 1]),
                        reads=(k_, "cst"), writes=(k_,))
                    P.op("scalar", lambda s, d=sc_: s.activation(
                        d[:, :], d[:, :], AF.Exp, bias=CZ, scale=-0.5),
                        reads=(k_,), writes=(k_,))
                for jj in range(4):
                    ba_ = pvp if jj < 2 else nmp
                    bk = "pv" if jj < 2 else "nm"
                    sc_, sk = (scT1, "scT1") if jj < 2 else (scT0, "scT0")
                    vr = 64 * (jj % 2) + 32 * (jj % 2)
                    zr = 64 * (jj % 2) + 32 * (1 - jj % 2)
                    P.op("vector", lambda v, j_=jj, ba=ba_, sc2=sc_, g_=g,
                         v_=vr, z_=zr:
                         v.tensor_tensor(
                             outc[32 * j_:32 * j_ + 32,
                                  g_ * QT:(g_ + 1) * QT],
                             ba[v_:v_ + 32, :],
                             sc2[z_:z_ + 32, :],
                             op=ALU.mult),
                         reads=(bk, sk), writes=(f"outc{g}",), deps=allg)
                yield
            yield "wo"
            for dc in range(NCH):
                sl = dc % 2
                load(x1r[:, sl * 512:(sl + 1) * 512],
                     x0_d[b, Q][dc * 128:(dc + 1) * 128, :], f"x1r{sl}",
                     f"x1r{sl}")
                bank, bap = (("pv", pvp) if dc % 2 == 0 else ("nm", nmp))
                fns = [(lambda t, g_=gg, c=dc, ba=bap, st=(gg == 0),
                        sp_=(gg == 3): t.matmul(
                            ba[:, :],
                            wo[:, g_ * D + c * 128:g_ * D + (c + 1) * 128],
                            outc[:, g_ * QT:(g_ + 1) * QT],
                            start=st, stop=sp_, skip_group_check=True), [])
                       for gg in range(4)]
                P.group("tensor", fns, writes=(bank,),
                        mreads=[(f"wo{gg}", f"outc{gg}") for gg in range(4)])
                P.op("vector", lambda v, c=dc, ba=bap, s_=sl:
                     v.scalar_tensor_tensor(
                         ybuf[:, c * (QT + 1) + 1:c * (QT + 1) + 1 + QT],
                         ba[:, :], CZ,
                         x1r[:, s_ * 512:(s_ + 1) * 512],
                         op0=ALU.bypass, op1=ALU.add),
                     reads=(bank, "cst", f"x1r{sl}"), writes=(f"yb{dc}",))
                if dc % 2 == 1:
                    yield
            yield from layer_norm(True, "pv", pvp, "nm", nmp,
                                  scT0, "scT0", scT1, "scT1",
                                  bfT0, "bfT0", bfT1, "bfT1", None, None)

        def layer_norm(inplace, bka, bpa, bkb, bpb, s0, k0, s1, k1,
                       ba, kba, bb, kbb, dst, dkeys):
            wda = P.wbegin((bka,))
            wdb = P.wbegin((bkb,))
            la = lb = None
            for cc in range(NCH):
                base = cc * (QT + 1) + 1
                cba = ba if cc % 2 == 0 else bfT2
                cbk = kba if cc % 2 == 0 else "bfT2"
                P.op("vector", lambda v, b_=base, dd=cba: v.tensor_copy(
                    dd[:, :], ybuf[:, b_:b_ + QT]),
                    reads=(f"yb{cc}",), writes=(cbk,))
                P.op("gpsimd", lambda v, dd=cba: v.tensor_tensor(
                    bb[:, :], dd[:, :], dd[:, :], op=ALU.mult),
                    reads=(cbk,), writes=(kbb,))
                la = P.op("tensor",
                          lambda t, st=(cc == 0), sp_=(cc == NCH - 1),
                          dd=cba:
                          t.matmul(bpa[:, :], onf[:, :], dd[:, :],
                                   start=st, stop=sp_,
                                   skip_group_check=True),
                          reads=("onf", cbk),
                          deps=(wda if cc == 0 else [la]))
                lb = P.op("tensor",
                          lambda t, st=(cc == 0), sp_=(cc == NCH - 1):
                          t.matmul(bpb[:, :], onf[:, :], bb[:, :],
                                   start=st, stop=sp_,
                                   skip_group_check=True),
                          reads=("onf", kbb),
                          deps=(wdb if cc == 0 else [lb]))
            P.wend(la, (bka,))
            P.wend(lb, (bkb,))
            P.op("scalar", lambda s: s.activation(
                s0[:, :], bpa[:, :], AF.Square, bias=CZ, scale=1.0 / 1024.0),
                reads=(bka, "cst"), writes=(k0,))
            P.op("vector", lambda v: v.scalar_tensor_tensor(
                s1[:, :], bpb[:, :], cst[:, CB_R1024:CB_R1024 + 1],
                s0[:, :], op0=ALU.mult, op1=ALU.subtract),
                reads=(bkb, "cst", k0), writes=(k1,))
            P.op("scalar", lambda s: s.activation(
                s1[:, :], s1[:, :], AF.Ln,
                bias=cst[:, CB_EPSL:CB_EPSL + 1]),
                reads=(k1, "cst"), writes=(k1,))
            P.op("scalar", lambda s: s.activation(
                s1[:, :], s1[:, :], AF.Exp, bias=CZ, scale=-0.5),
                reads=(k1,), writes=(k1,))
            P.op("vector", lambda v: v.scalar_tensor_tensor(
                s0[:, :], bpa[:, :], cst[:, CB_RN1024:CB_RN1024 + 1],
                s1[:, :], op0=ALU.mult, op1=ALU.mult),
                reads=(bka, "cst", k1), writes=(k0,))
            yield
            for cc in range(NCH):
                base = cc * (QT + 1) + 1
                if inplace:
                    da, dk = ybuf[:, base:base + QT], f"yb{cc}"
                    P.op("vector", lambda v, b_=base, dd=da: v.tensor_tensor(
                        dd, ybuf[:, b_:b_ + QT], s1[:, :], op=ALU.mult),
                        reads=(dk, k1), writes=(dk,))
                    P.op("vector", lambda v, dd=da: v.tensor_tensor(
                        dd, dd, s0[:, :], op=ALU.add),
                        reads=(dk, k0), writes=(dk,))
                else:
                    da, dk = ybuf[:, base:base + QT], f"yb{cc}"
                    P.op("gpsimd", lambda v, dd=da: v.tensor_tensor(
                        dd, dd, s1[:, :], op=ALU.mult),
                        reads=(dk, k1), writes=(dk,))
                    P.op("gpsimd", lambda v, dd=da: v.tensor_tensor(
                        dd, dd, s0[:, :], op=ALU.add),
                        reads=(dk, k0), writes=(dk,))
                    so = P.op("sync",
                              lambda s, b_=layer_norm.b, q_=layer_norm.q,
                              c=cc, dd=da: s.dma_start(
                                  out_d[b_, q_][c * 128:(c + 1) * 128, :],
                                  dd),
                              reads=(dk,), dma=f"or{cc % 2}")
                    store_ops.append(so)
                if cc % 4 == 3:
                    yield

        def stF(b, Q):
            wrtr = tuple(f"wrt{c}" for c in range(NCH))
            wktr = tuple(f"wkt{c}" for c in range(NCH))
            wvtr = tuple(f"wvt{c}" for c in range(NCH))
            xr_ops, xk_ops, kact_ops = [], [], []
            for cc in range(NCH):
                base = cc * (QT + 1)
                dsb = bfF0 if cc % 2 == 0 else bfF2
                dsk = "bfF0" if cc % 2 == 0 else "bfF2"
                P.op("vector", lambda v, b_=base, dd=dsb: v.tensor_tensor(
                    dd[:, :], ybuf[:, b_ + 1:b_ + 1 + QT],
                    ybuf[:, b_:b_ + QT], op=ALU.subtract),
                    reads=(f"yb{cc}",), writes=(dsk,))
                xr_ops.append(P.op("vector", lambda v, c=cc, b_=base,
                     dd=dsb: v.scalar_tensor_tensor(
                         xr[:, c * QT:(c + 1) * QT], dd[:, :],
                         mur[:, c:c + 1], ybuf[:, b_:b_ + QT],
                         op0=ALU.mult, op1=ALU.add),
                     reads=(dsk, "mur", f"yb{cc}"), writes=(f"xr{cc}",)))
                xk_ops.append(P.op("vector", lambda v, c=cc, b_=base,
                     dd=dsb: v.scalar_tensor_tensor(
                         xk[:, c * QT:(c + 1) * QT], dd[:, :],
                         muk[:, c:c + 1], ybuf[:, b_:b_ + QT],
                         op0=ALU.mult, op1=ALU.add),
                     reads=(dsk, "muk", f"yb{cc}"), writes=(f"xk{cc}",)))
                if cc % 4 == 3:
                    yield
            if Q == n_quarters - 1:
                for cc in range(NCH):
                    P.op("vector", lambda v, c=cc: v.memset(
                        ybuf[:, c * (QT + 1):c * (QT + 1) + 1], 0.0),
                        writes=(f"yb{cc}",))
            else:
                for cc in range(NCH):
                    P.op("vector", lambda v, c=cc: v.tensor_copy(
                        ybuf[:, c * (QT + 1):c * (QT + 1) + 1],
                        ybuf[:, c * (QT + 1) + QT:c * (QT + 1) + QT + 1]),
                        reads=(f"yb{cc}",), writes=(f"yb{cc}",))
            yield
            for dc in range(NCH):
                fns = [(lambda t, c=cc_, d_=dc, st=(cc_ == 0),
                        sp_=(cc_ == NCH - 1): t.matmul(
                            gap[:, :],
                            wrt[:, c * D + d_ * 128:c * D + (d_ + 1) * 128],
                            xr[:, c * QT:(c + 1) * QT],
                            start=st, stop=sp_, skip_group_check=True),
                        [xr_ops[cc_]])
                       for cc_ in range(NCH)]
                P.group("tensor", fns, writes=("ga",),
                        mreads=[(f"wrt{c}", f"xr{c}") for c in range(NCH)])
                P.op("vector", lambda v, c=dc: v.tensor_copy(
                    esig[:, c * QT:(c + 1) * QT], gap[:, :]),
                    reads=("ga",), writes=(f"esig{dc}",))
                if dc % 2 == 1:
                    P.op("scalar", lambda s, c=dc: s.activation(
                        esig[:, (c - 1) * QT:(c + 1) * QT],
                        esig[:, (c - 1) * QT:(c + 1) * QT], AF.Exp,
                        bias=CZ, scale=-1.0),
                        reads=(f"esig{dc - 1}", f"esig{dc}", "cst"),
                        writes=(f"esig{dc - 1}", f"esig{dc}"))
                yield
                fns = [(lambda t, c=cc_, d_=dc, st=(cc_ == 0),
                        sp_=(cc_ == NCH - 1): t.matmul(
                            gbp[:, :],
                            wkt[:, c * D + d_ * 128:c * D + (d_ + 1) * 128],
                            xk[:, c * QT:(c + 1) * QT],
                            start=st, stop=sp_, skip_group_check=True),
                        [xk_ops[cc_]])
                       for cc_ in range(NCH)]
                P.group("tensor", fns, writes=("gb",),
                        mreads=[(f"wkt{c}", f"xk{c}") for c in range(NCH)])
                P.op("vector", lambda v: v.tensor_scalar_max(
                    bfF1[:, :], gbp[:, :], CZ),
                    reads=("gb", "cst"), writes=("bfF1",))
                kact_ops.append(P.op("vector", lambda v, c=dc: v.tensor_tensor(
                    kact[:, c * QT:(c + 1) * QT], bfF1[:, :],
                    bfF1[:, :], op=ALU.mult),
                    reads=("bfF1",), writes=(f"kact{dc}",)))
                yield
            for dc2 in range(NCH):
                bank, bap = (("ga", gap) if dc2 % 2 == 0 else ("gb", gbp))
                fns = [(lambda t, c=cc_, d_=dc2, ba=bap, st=(cc_ == 0),
                        sp_=(cc_ == NCH - 1): t.matmul(
                            ba[:, :],
                            wvt[:, c * D + d_ * 128:c * D + (d_ + 1) * 128],
                            kact[:, c * QT:(c + 1) * QT],
                            start=st, stop=sp_, skip_group_check=True),
                        [kact_ops[cc_]])
                       for cc_ in range(NCH)]
                P.group("tensor", fns, writes=(bank,),
                        mreads=[(f"wvt{c}", f"kact{c}") for c in range(NCH)])
                P.op("vector", lambda v, c=dc2: v.tensor_scalar_add(
                    scF0[:, :], esig[:, c * QT:(c + 1) * QT],
                    cst[:, CB_ONE:CB_ONE + 1]),
                    reads=(f"esig{dc2}", "cst"), writes=("scF0",))
                P.op("scalar", lambda s: s.activation(
                    scF0[:, :], scF0[:, :], AF.Ln, bias=CZ),
                    reads=("scF0", "cst"), writes=("scF0",))
                P.op("scalar", lambda s: s.activation(
                    scF0[:, :], scF0[:, :], AF.Exp, bias=CZ, scale=-1.0),
                    reads=("scF0",), writes=("scF0",))
                P.op("vector", lambda v, ba=bap: v.scalar_tensor_tensor(
                    scF0[:, :], ba[:, :], CZ, scF0[:, :],
                    op0=ALU.bypass, op1=ALU.mult),
                    reads=(bank, "cst", "scF0"), writes=("scF0",))
                P.op("vector", lambda v, c=dc2: v.tensor_tensor(
                    ybuf[:, c * (QT + 1) + 1:c * (QT + 1) + 1 + QT],
                    ybuf[:, c * (QT + 1) + 1:c * (QT + 1) + 1 + QT],
                    scF0[:, :], op=ALU.add),
                    reads=(f"yb{dc2}", "scF0"), writes=(f"yb{dc2}",))
                yield
            layer_norm.b, layer_norm.q = b, Q
            yield from layer_norm(False, "ga", gap, "gb", gbp,
                                  scF0, "scF0", scF1, "scF1",
                                  bfF0, "bfF0", bfF1, "bfF1",
                                  None, None)

        genA = [stA(b, q) for b in range(b_core) for q in range(n_quarters)]
        genT = [stT(b, q) for b in range(b_core) for q in range(n_quarters)]
        genF = [stF(b, q) for b in range(b_core) for q in range(n_quarters)]
        NS = b_core * n_quarters

        def n_units_T(i):
            return 4 * (4 * ((i % n_quarters) + 1) + 1) + 8 + 3
        for s in range(NS + 2):
            drive(genF[s - 2] if 0 <= s - 2 < NS else None,
                  genT[s - 1] if 0 <= s - 1 < NS else None,
                  genA[s] if s < NS else None,
                  nf=28, nt=n_units_T(s - 1) if 1 <= s <= NS else 1, na=19)

        P.op("sync", None, deps=tuple(store_ops))

        dma_c_ops = [o for o in P.all if o.dma == "dma_c"]
        for o in P.all:
            if o.dma is None and any(d.dma == "dma_c" for d in o.deps):
                have = set(map(id, o.deps))
                o.deps = tuple(o.deps) + tuple(
                    x for x in dma_c_ops if id(x) not in have)
        P.schedule()
        P.finalize()
        nc._prog = P
        for o in P.all:
            if o.dma == "dma_c":
                o.count = 16 * 8
        for e in ENGINES:
            for o in P.ops[e]:
                o.waits = [(c, (16 * 8 if c == "dma_c" else v))
                           for (c, v) in o.waits]
        blk = EC(nc.Block())

        def mk(eng_name):
            def emitter(eng):
                for o in P.ops[eng_name]:
                    for (pp, c) in o.waits:
                        eng.wait_ge(sems[pp], c)
                    if o.fn is None:
                        continue
                    instr = o.fn(eng)
                    if o.inc:
                        instr.then_inc(sems[o.chan], 16 if o.dma else 1)
            return emitter

        blk.sync(mk("sync"))
        blk.tensor(mk("tensor"))
        blk.vector(mk("vector"))
        blk.scalar(mk("scalar"))
        blk.gpsimd(mk("gpsimd"))

    return nc



def _prep_inputs(L0_attn_x, L0_attn_Wq, L0_attn_Wkv, L0_attn_Wk, L0_attn_Wv,
                 L0_attn_Wo, L0_attn_q_g, L0_attn_k_g, L0_ffn_W_r,
                 L0_ffn_W_k, L0_ffn_W_v, L0_ffn_mu_r, L0_ffn_mu_k):
    n_quarters = 4
    wq = np.asarray(L0_attn_Wq, dtype=np.float32)
    wq_r = wq.reshape(D, H, HD)
    wq_perm = np.concatenate([wq_r[:, :, :ROPE].reshape(D, H * ROPE),
                              wq_r[:, :, ROPE:].reshape(D, H * ROPE)], axis=1)
    wo = np.asarray(L0_attn_Wo, dtype=np.float32)
    wo_sub = np.ascontiguousarray(
        wo.reshape(H, HD, D)[:, :ROPE, :].reshape(H * ROPE, D))
    shared = {
        "wq": np.ascontiguousarray(wq_perm),
        "wkv": np.ascontiguousarray(np.asarray(L0_attn_Wkv, np.float32)),
        "wk": np.ascontiguousarray(np.asarray(L0_attn_Wk, np.float32)),
        "wv": np.ascontiguousarray(np.asarray(L0_attn_Wv, np.float32)),
        "wo": wo_sub,
        "wrt": np.ascontiguousarray(np.asarray(L0_ffn_W_r, np.float32).T),
        "wkt": np.ascontiguousarray(np.asarray(L0_ffn_W_k, np.float32).T),
        "wvt": np.ascontiguousarray(np.asarray(L0_ffn_W_v, np.float32).T),
        "mur": np.ascontiguousarray(
            np.asarray(L0_ffn_mu_r, np.float32).reshape(NCH, 128).T),
        "muk": np.ascontiguousarray(
            np.asarray(L0_ffn_mu_k, np.float32).reshape(NCH, 128).T),
        "qg": np.ascontiguousarray(
            np.tile(np.asarray(L0_attn_q_g, np.float32)[:ROPE], 4)
            .reshape(128, 1)),
        "kg": np.ascontiguousarray(
            np.tile(np.asarray(L0_attn_k_g, np.float32), 4).reshape(128, 1)),
        "cst": np.tile(np.array([CONST_VALS], np.float32), (128, 1)),
        "tri": np.ascontiguousarray(np.tile(
            (np.arange(128)[None, :] >= np.arange(128)[:, None])
            .astype(bfloat16), (1, 2))),
        "obd": np.ascontiguousarray(
            (np.arange(128)[:, None] // 32
             == np.arange(128)[None, :] // 32).astype(bfloat16)),
        "onf": np.ones((128, 128), dtype=bfloat16),
    }
    x = np.asarray(L0_attn_x, np.float32)
    bsz, t, d = x.shape
    b_core = bsz // N_CORES
    in_maps = []
    for c in range(N_CORES):
        xs = x[c * b_core:(c + 1) * b_core]
        xq = xs.reshape(b_core, n_quarters, QT, d)
        xq = np.ascontiguousarray(xq.transpose(0, 1, 3, 2))
        in_maps.append(dict(shared, x0=xq))
    return in_maps, b_core, n_quarters


def kernel(**inputs):
    in_maps, b_core, n_quarters = _prep_inputs(**inputs)
    nc = build_program(n_quarters=n_quarters, b_core=b_core)
    res = run_bass_kernel_spmd(nc, in_maps, core_ids=list(range(N_CORES)))
    outs = []
    for c in range(N_CORES):
        o = np.asarray(res.results[c]["out"])
        o = o.transpose(0, 1, 3, 2).reshape(b_core, n_quarters * QT, D)
        outs.append(o)
    return np.ascontiguousarray(np.concatenate(outs, axis=0))


# version 24 (final)
# speedup vs baseline: 1.0302x; 1.0302x over previous
import contextlib

import numpy as np
import ml_dtypes

import concourse.bass as bass
import concourse.mybir as mybir
from concourse.bass_utils import run_bass_kernel_spmd

F32 = mybir.dt.float32
BF16 = mybir.dt.bfloat16
AF = mybir.ActivationFunctionType
ALU = mybir.AluOpType
bfloat16 = ml_dtypes.bfloat16

D = 1024
H = 16
HD = 64
ROPE = 32
LAT = 128
NCH = 8
QT = 512
KB = 128
N_CORES = 8

(CB_EXPB, CB_EPSR, CB_EPSL, CB_ZERO, CB_ONE, CB_R1024, CB_RN1024,
 CB_TINY) = range(8)
CONST_VALS = [-6.0, 1e-6, 1e-5, 0.0, 1.0, 1.0 / 1024.0, -1.0 / 1024.0,
              1e-30]

ENGINES = ("sync", "tensor", "vector", "scalar", "gpsimd")


class Op:
    __slots__ = ("eng", "fn", "deps", "inc", "count", "waits", "dma",
                 "chan", "rw")

    def __init__(self, eng, fn, deps, dma):
        self.rw = None
        self.eng = eng
        self.fn = fn
        self.deps = deps
        self.dma = dma
        self.chan = None
        self.inc = False
        self.count = None
        self.waits = []


class Prog:

    def __init__(self):
        self.ops = {e: [] for e in ENGINES}
        self.all = []
        self.bufs = {}

    def _pre(self, reads, writes):
        deps = []
        for k in reads:
            st = self.bufs.get(k)
            if st and st["w"] is not None:
                deps.append(st["w"])
        for k in writes:
            st = self.bufs.get(k)
            if st:
                if st["r"]:
                    deps.extend(st["r"])
                elif st["w"] is not None:
                    deps.append(st["w"])
        return deps

    def _post(self, op, reads, writes):
        for k in reads:
            self.bufs.setdefault(k, {"w": None, "r": []})["r"].append(op)
        for k in writes:
            self.bufs[k] = {"w": op, "r": []}

    def op(self, eng, fn, reads=(), writes=(), deps=(), dma=None):
        d = self._pre(reads, writes) + [x for x in deps if x is not None]
        o = Op(eng, fn, d, dma)
        o.rw = (tuple(reads), tuple(writes))
        self.ops[eng].append(o)
        self.all.append(o)
        self._post(o, reads, writes)
        return o

    def group(self, eng, fns, reads=(), writes=(), deps0=(), mreads=None):
        d0 = self._pre(reads, writes) + [x for x in deps0 if x is not None]
        last = None
        regs = list(reads)
        for i, (fn, extra) in enumerate(fns):
            dd = (d0 if i == 0 else []) + [x for x in extra if x is not None]
            mr = ()
            if mreads is not None:
                mr = tuple(mreads[i])
                regs.extend(mr)
                for k in mr:
                    st = self.bufs.get(k)
                    if st and st["w"] is not None:
                        dd.append(st["w"])
            if last is not None:
                dd = dd + [last]
            nxt = Op(eng, fn, dd, False)
            nxt.rw = (tuple(reads) + mr, tuple(writes))
            self.ops[eng].append(nxt)
            self.all.append(nxt)
            last = nxt
        self._post(last, tuple(regs), writes)
        return last

    def wbegin(self, keys):
        return self._pre((), keys)

    def wend(self, op, keys):
        for k in keys:
            self.bufs[k] = {"w": op, "r": []}

    def est_cost(self, o):
        if o.dma:
            return 790.0
        if o.eng == "tensor":
            rw = o.rw[0] + o.rw[1]
            if any(k.startswith(("eb", "vtm", "ks")) for k in rw) or \
               "onf" in o.rw[0][:1]:
                return 110.0
            return 240.0
        if o.eng == "scalar":
            return 1000.0 if any(k.startswith("eb") for k in o.rw[1]) \
                else 650.0
        if o.eng == "gpsimd":
            return 1300.0
        return 620.0

    def schedule(self):
        import heapq
        PROP = 200.0
        DMA_LAT = 600.0
        succs = {id(o): [] for o in self.all}
        indeg = {id(o): 0 for o in self.all}
        for o in self.all:
            for d in o.deps:
                succs[id(d)].append(o)
                indeg[id(o)] += 1
        prio = {}
        for o in reversed(self.all):
            c = self.est_cost(o)
            best = 0.0
            for s in succs[id(o)]:
                best = max(best, prio[id(s)])
            prio[id(o)] = c + best

        avail = {e: 0.0 for e in ENGINES}
        done_t = {}
        ready = {e: [] for e in ENGINES}
        seq = {e: 0 for e in ENGINES}
        emit_idx = {id(o): i for i, o in enumerate(self.all)}
        for o in self.all:
            if indeg[id(o)] == 0:
                heapq.heappush(ready[o.eng],
                               (-prio[id(o)], emit_idx[id(o)], id(o), o))
        new_ops = {e: [] for e in ENGINES}
        n_done = 0
        pending = []
        def start_time(o, e):
            t0 = avail[e]
            for d in o.deps:
                t0 = max(t0, done_t[id(d)] +
                         (0.0 if d.eng == o.eng and not d.dma else PROP))
            return t0
        while n_done < len(self.all):
            best = None
            for e in ENGINES:
                if not ready[e]:
                    continue
                cand = heapq.nsmallest(8, ready[e])
                eb = None
                for item in cand:
                    o = item[3]
                    t0 = start_time(o, e)
                    if eb is None or t0 < eb[0] - 200.0 or \
                       (abs(t0 - eb[0]) <= 200.0 and
                        prio[id(o)] > prio[id(eb[1])]):
                        eb = (t0, o, item)
                if best is None or eb[0] < best[0] or \
                   (eb[0] == best[0] and prio[id(eb[1])] > prio[id(best[1])]):
                    best = (eb[0], eb[1], e, eb[2])
            t0, o, e, item = best
            ready[e].remove(item)
            heapq.heapify(ready[e])
            c = self.est_cost(o)
            if o.dma:
                avail[e] = t0 + c
                fin = t0 + c + DMA_LAT
            else:
                avail[e] = t0 + c
                fin = t0 + c
            done_t[id(o)] = fin
            new_ops[e].append(o)
            n_done += 1
            for s in succs[id(o)]:
                indeg[id(s)] -= 1
                if indeg[id(s)] == 0:
                    heapq.heappush(ready[s.eng],
                                   (-prio[id(s)], emit_idx[id(s)], id(s), s))
        self.ops = new_ops

    def finalize(self):
        for o in self.all:
            for d in o.deps:
                if d.eng != o.eng or d.dma:
                    d.inc = True
        cnt = {}
        for e in ENGINES:
            for o in self.ops[e]:
                o.chan = o.dma if o.dma else e
                if o.dma:
                    o.inc = True
                if o.inc:
                    cnt[o.chan] = cnt.get(o.chan, 0) + (16 if o.dma else 1)
                    o.count = cnt[o.chan]
        hwm = {e: {} for e in ENGINES}
        for e in ENGINES:
            for o in self.ops[e]:
                for d in o.deps:
                    if d.chan == e and not d.dma:
                        continue
                    if d.count > hwm[e].get(d.chan, 0):
                        o.waits.append((d.chan, d.count))
                        hwm[e][d.chan] = d.count


def drive(gf, gt, ga, nf=1, nt=1, na=1):
    rem = {}
    for g, n in ((gf, nf), (gt, nt), (ga, na)):
        if g is not None:
            rem[g] = max(1, n)
    t_blocked = False
    while rem:
        order = sorted(rem, key=lambda g: -rem[g])
        g = order[0]
        if g is gt and t_blocked:
            others = [x for x in order[1:]]
            if gf in rem:
                g = others[0] if others else None
                if g is None:
                    continue
            else:
                t_blocked = False
        try:
            v = next(g)
            rem[g] = max(1, rem[g] - 1)
            if g is gt and v == "wo" and gf in rem:
                t_blocked = True
        except StopIteration:
            del rem[g]


def build_program(n_quarters=4, b_core=2):
    T = n_quarters * QT
    NKC = T // KB

    nc = bass.Bass("TRN2", target_bir_lowering=False,
                   dynamic_dma_scratch_size=512,
                   detect_race_conditions=False)

    x0_d = nc.dram_tensor("x0", [b_core, n_quarters, D, QT], F32,
                          kind="ExternalInput").ap()
    out_d = nc.dram_tensor("out", [b_core, n_quarters, D, QT], F32,
                           kind="ExternalOutput").ap()
    w_d = {}
    for nm, shp in [("wq", [D, D]), ("wkv", [D, LAT]),
                    ("wk", [LAT, H * ROPE]), ("wv", [LAT, H * ROPE]),
                    ("wo", [H * ROPE, D]), ("wrt", [D, D]),
                    ("wkt", [D, D]), ("wvt", [D, D])]:
        w_d[nm] = nc.dram_tensor(nm, shp, F32, kind="ExternalInput").ap()
    mur_d = nc.dram_tensor("mur", [128, NCH], F32, kind="ExternalInput").ap()
    muk_d = nc.dram_tensor("muk", [128, NCH], F32, kind="ExternalInput").ap()
    qg_d = nc.dram_tensor("qg", [128, 1], F32, kind="ExternalInput").ap()
    kg_d = nc.dram_tensor("kg", [128, 1], F32, kind="ExternalInput").ap()
    cst_d = nc.dram_tensor("cst", [128, len(CONST_VALS)], F32,
                           kind="ExternalInput").ap()
    tri_d = nc.dram_tensor("tri", [128, 256], BF16, kind="ExternalInput").ap()
    obd_d = nc.dram_tensor("obd", [128, 128], BF16, kind="ExternalInput").ap()
    onf_d = nc.dram_tensor("onf", [128, 128], BF16, kind="ExternalInput").ap()

    with contextlib.ExitStack() as es:
        EC = es.enter_context

        wq = EC(nc.sbuf_tensor("wqs", [128, NCH * D], BF16))
        wkv = EC(nc.sbuf_tensor("wkvs", [128, NCH * LAT], BF16))
        wk = EC(nc.sbuf_tensor("wks", [128, H * ROPE], BF16))
        wv = EC(nc.sbuf_tensor("wvs", [128, H * ROPE], BF16))
        wo = EC(nc.sbuf_tensor("wos", [128, 4 * D], BF16))
        wrt = EC(nc.sbuf_tensor("wrts", [128, NCH * D], BF16))
        wkt = EC(nc.sbuf_tensor("wkts", [128, NCH * D], BF16))
        wvt = EC(nc.sbuf_tensor("wvts", [128, NCH * D], BF16))
        mur = EC(nc.sbuf_tensor("murs", [128, NCH], F32))
        muk = EC(nc.sbuf_tensor("muks", [128, NCH], F32))
        qg = EC(nc.sbuf_tensor("qgs", [128, 1], F32))
        kg = EC(nc.sbuf_tensor("kgs", [128, 1], F32))
        cst = EC(nc.sbuf_tensor("csts", [128, len(CONST_VALS)], F32))
        tri = EC(nc.sbuf_tensor("tris", [128, 256], BF16))
        obd = EC(nc.sbuf_tensor("obds", [128, 128], BF16))
        onf = EC(nc.sbuf_tensor("onfs", [128, 128], BF16))
        ybuf = EC(nc.sbuf_tensor("ybuf", [128, NCH * (QT + 1)], F32))
        x0bf = EC(nc.sbuf_tensor("x0bf", [128, NCH * QT], BF16))
        xr = EC(nc.sbuf_tensor("xrb", [128, NCH * QT], BF16))
        xk = EC(nc.sbuf_tensor("xkb", [128, NCH * QT], BF16))
        qs2 = EC(nc.sbuf_tensor("qs2", [128, 2 * 4 * QT], BF16))
        ks = EC(nc.sbuf_tensor("ksb", [128, 4 * T], BF16))
        kvbf = EC(nc.sbuf_tensor("kvbf", [128, QT], BF16))
        ebuf = EC(nc.sbuf_tensor("ebuf", [128, 2 * 2048], BF16))
        outc = EC(nc.sbuf_tensor("outcb", [128, 4 * QT], BF16))
        esig = EC(nc.sbuf_tensor("esig", [128, NCH * QT], BF16))
        kact = EC(nc.sbuf_tensor("kactb", [128, NCH * QT], BF16))
        x0r = EC(nc.sbuf_tensor("x0r", [128, 2 * 512], F32))
        x1r = EC(nc.sbuf_tensor("x1r", [128, 2 * 512], F32))
        scA1 = EC(nc.sbuf_tensor("scA1", [128, 512], F32))
        scT0 = EC(nc.sbuf_tensor("scT0", [128, 512], F32))
        scT1 = EC(nc.sbuf_tensor("scT1", [128, 512], F32))
        scF0 = EC(nc.sbuf_tensor("scF0", [128, 512], F32))
        scF1 = EC(nc.sbuf_tensor("scF1", [128, 512], F32))
        bfA0 = EC(nc.sbuf_tensor("bfA0", [128, 512], BF16))
        bfA1 = EC(nc.sbuf_tensor("bfA1", [128, 512], BF16))
        bfA2 = EC(nc.sbuf_tensor("bfA2", [128, 512], BF16))
        bfT0 = EC(nc.sbuf_tensor("bfT0", [128, 512], BF16))
        bfT1 = EC(nc.sbuf_tensor("bfT1", [128, 512], BF16))
        bfF0 = EC(nc.sbuf_tensor("bfF0", [128, 512], BF16))
        bfF1 = EC(nc.sbuf_tensor("bfF1", [128, 512], BF16))
        bfF2 = EC(nc.sbuf_tensor("bfF2", [128, 512], BF16))
        bfT2 = EC(nc.sbuf_tensor("bfT2", [128, 512], BF16))
        vtm = EC(nc.sbuf_tensor("vtm", [128, NKC * 768], BF16))

        sp = EC(nc.psum_tensor("spp", [128, 2048], F32))
        pvp = EC(nc.psum_tensor("pvp", [128, 512], F32))
        nmp = EC(nc.psum_tensor("nmp", [128, 512], F32))
        gap = EC(nc.psum_tensor("gap", [128, 512], F32))
        gbp = EC(nc.psum_tensor("gbp", [128, 512], F32))

        DMA_CHANS = ("dma_c", "dma_w0", "dma_w1", "x0r0", "x0r1",
                     "x1r0", "x1r1", "or0", "or1")
        sems = {e: EC(nc.semaphore("sem_" + e))
                for e in list(ENGINES) + list(DMA_CHANS)}
        P = Prog()

        CZ = cst[:, CB_ZERO:CB_ZERO + 1]

        def load(dst_ap, src_ap, wkey, chan):
            return P.op("sync",
                        lambda s, dd=dst_ap, rr=src_ap: s.dma_start(dd, rr),
                        writes=(wkey,), dma=chan)

        load(cst[:, :], cst_d, "cst", "dma_c")
        load(mur[:, :], mur_d, "mur", "dma_c")
        load(muk[:, :], muk_d, "muk", "dma_c")
        load(qg[:, :], qg_d, "qg", "dma_c")
        load(kg[:, :], kg_d, "kg", "dma_c")
        load(tri[:, :], tri_d, "tri", "dma_c")
        load(obd[:, :], obd_d, "obd", "dma_c")
        load(onf[:, :], onf_d, "onf", "dma_c")

        wslot = [0]

        def wload(name, n_chunks, cols, dst, dst_cols):
            for ch in range(n_chunks):
                prev = None
                for p0 in range(0, cols, 256):
                    pw = min(256, cols - p0)
                    sl = wslot[0] % 2
                    wslot[0] += 1
                    src = w_d[name][ch * 128:(ch + 1) * 128, p0:p0 + pw]
                    P.op("sync",
                         lambda s, sr=src, w=pw, s_=sl: s.dma_start(
                             scF1[:, s_ * 256:s_ * 256 + w], sr),
                         writes=(f"stage{sl}",), dma=f"dma_w{sl}")
                    last = p0 + pw >= cols
                    prev = P.op(
                        "vector",
                        lambda v, c=ch, w=pw, q0=p0, dd=dst, dc=dst_cols,
                        s_=sl:
                        v.tensor_copy(dd[:, c * dc + q0:c * dc + q0 + w],
                                      scF1[:, s_ * 256:s_ * 256 + w]),
                        reads=(f"stage{sl}",),
                        writes=(f"{name}{ch}",) if last else (),
                        deps=(prev,))

        wload("wq", NCH, D, wq, D)
        _stage_fence_pending = True
        wload("wkv", NCH, LAT, wkv, LAT)
        wload("wk", 1, H * ROPE, wk, H * ROPE)
        wload("wv", 1, H * ROPE, wv, H * ROPE)
        wload("wo", 4, D, wo, D)
        wload("wrt", NCH, D, wrt, D)
        wload("wkt", NCH, D, wkt, D)
        wload("wvt", NCH, D, wvt, D)
        P.op("vector", lambda v: v.memset(scF1[0:1, 0:1], 0.0),
             writes=("stage0", "stage1", "scF1"))

        P.op("vector", lambda v: v.tensor_copy(
            bass.AP(vtm.ap().tensor, 32,
                    [[NKC * 768, 128], [96, NKC * 8], [1, 32]]),
            bass.AP(onf.ap().tensor, 0,
                    [[128, 128], [0, NKC * 8], [1, 32]])),
            reads=("onf",), writes=("vones",))

        for cc in range(NCH):
            P.op("vector", lambda v, c=cc: v.memset(
                ybuf[:, c * (QT + 1):c * (QT + 1) + 1], 0.0),
                writes=(f"yb{cc}",))

        store_ops = []


        def stA(b, Q):
            par = Q % 2
            qb0 = par * 4 * QT
            qkey = f"qs{par}"
            cast_ops = []
            for w in range(4):
                for sl in range(2):
                    cc = 2 * w + sl
                    load(x0r[:, sl * 512:(sl + 1) * 512],
                         x0_d[b, Q][cc * 128:(cc + 1) * 128, :], f"x0r{sl}",
                         f"x0r{sl}")
                cw = P.op(
                    "vector", lambda v, w_=w: v.tensor_copy(
                        x0bf[:, w_ * 2 * QT:(w_ + 1) * 2 * QT],
                        x0r[:, :]),
                    reads=("x0r0", "x0r1"),
                    writes=tuple(f"x0bf{2 * w + s}" for s in range(2)))
                cast_ops.extend([cw] * 2)
                if w % 2 == 1:
                    yield
            wqr = tuple(f"wq{c}" for c in range(NCH))
            x0keys = tuple(f"x0bf{c}" for c in range(NCH))
            for g in range(4):
                fns = [(lambda t, c=cc_, o=4 + g, st=(cc_ == 0),
                        sp_=(cc_ == NCH - 1): t.matmul(
                            gap[:, :],
                            wq[:, c * D + o * 128:c * D + (o + 1) * 128],
                            x0bf[:, c * QT:(c + 1) * QT],
                            start=st, stop=sp_, skip_group_check=True),
                        [cast_ops[cc_]])
                       for cc_ in range(NCH)]
                P.group("tensor", fns, writes=("ga",),
                        mreads=[(f"wq{c}", f"x0bf{c}") for c in range(NCH)])
                P.op("vector", lambda v: v.tensor_copy(bfA0[:, :],
                                                       gap[:, :]),
                    reads=("ga",), writes=("bfA0",))
                P.op("vector", lambda v: v.tensor_tensor(
                    bfA2[:, :], bfA0[:, :], bfA0[:, :], op=ALU.mult),
                    reads=("bfA0",), writes=("bfA2",))
                yield
                fns = [(lambda t, c=cc_, o=g, st=(cc_ == 0),
                        sp_=(cc_ == NCH - 1): t.matmul(
                            gbp[:, :],
                            wq[:, c * D + o * 128:c * D + (o + 1) * 128],
                            x0bf[:, c * QT:(c + 1) * QT],
                            start=st, stop=sp_, skip_group_check=True),
                        [cast_ops[cc_]])
                       for cc_ in range(NCH)]
                P.group("tensor", fns, writes=("gb",),
                        mreads=[(f"wq{c}", f"x0bf{c}") for c in range(NCH)])
                P.op("vector", lambda v: v.tensor_copy(bfA1[:, :],
                                                       gbp[:, :]),
                    reads=("gb",), writes=("bfA1",))
                wdg = P.wbegin(("ga",))
                m1 = P.op("tensor", lambda t: t.matmul(
                    gap[:, :], obd[:, :], bfA2[:, :], start=True,
                    stop=False, skip_group_check=True),
                    reads=("obd", "bfA2"), deps=wdg)
                P.op("vector", lambda v: v.tensor_tensor(
                    bfA2[:, :], bfA1[:, :], bfA1[:, :], op=ALU.mult),
                    reads=("bfA1",), writes=("bfA2",))
                m2 = P.op("tensor", lambda t: t.matmul(
                    gap[:, :], obd[:, :], bfA2[:, :], start=False,
                    stop=True, skip_group_check=True),
                    reads=("obd", "bfA2"))
                P.wend(m2, ("ga",))
                P.op("vector", lambda v: v.tensor_copy(scA1[:, :],
                                                       gap[:, :]),
                    reads=("ga",), writes=("scA1",))
                P.op("scalar", lambda s: s.activation(
                    scA1[:, :], scA1[:, :], AF.Ln,
                    bias=cst[:, CB_EPSR:CB_EPSR + 1], scale=1.0 / 64.0),
                    reads=("scA1", "cst"), writes=("scA1",))
                P.op("scalar", lambda s: s.activation(
                    scA1[:, :], scA1[:, :], AF.Exp, bias=CZ, scale=-0.5),
                    reads=("scA1",), writes=("scA1",))
                P.op("vector", lambda v, g_=g: v.scalar_tensor_tensor(
                    qs2[:, qb0 + g_ * QT:qb0 + (g_ + 1) * QT], bfA1[:, :],
                    qg[:, 0:1], scA1[:, :], op0=ALU.mult, op1=ALU.mult),
                    reads=("bfA1", "qg", "scA1"), writes=(f"{qkey}_{g}",))
                yield
            fns = [(lambda t, c=cc_, st=(cc_ == 0), sp_=(cc_ == NCH - 1):
                    t.matmul(gap[:, :], wkv[:, c * LAT:(c + 1) * LAT],
                             x0bf[:, c * QT:(c + 1) * QT],
                             start=st, stop=sp_, skip_group_check=True),
                    [cast_ops[cc_]])
                   for cc_ in range(NCH)]
            P.group("tensor", fns, writes=("ga",),
                    mreads=[(f"wkv{c}", f"x0bf{c}") for c in range(NCH)])
            P.op("vector", lambda v: v.tensor_copy(kvbf[:, :], gap[:, :]),
                 reads=("ga",), writes=("kvbf",))
            yield
            for g in range(4):
                P.group("tensor",
                        [(lambda t, g_=g: t.matmul(
                            gap[:, :], wk[:, g_ * 128:(g_ + 1) * 128],
                            kvbf[:, :], start=True, stop=True,
                            skip_group_check=True), [])],
                        reads=("wk0", "kvbf"), writes=("ga",))
                P.op("vector", lambda v: v.tensor_copy(bfA1[:, :],
                                                       gap[:, :]),
                    reads=("ga",), writes=("bfA1",))
                P.op("vector", lambda v: v.tensor_tensor(
                    bfA2[:, :], bfA1[:, :], bfA1[:, :], op=ALU.mult),
                    reads=("bfA1",), writes=("bfA2",))
                P.group("tensor",
                        [(lambda t: t.matmul(
                            gbp[:, :], obd[:, :], bfA2[:, :],
                            start=True, stop=True,
                            skip_group_check=True), [])],
                        reads=("obd", "bfA2"), writes=("gb",))
                P.op("vector", lambda v: v.tensor_copy(scA1[:, :],
                                                       gbp[:, :]),
                    reads=("gb",), writes=("scA1",))
                P.op("scalar", lambda s: s.activation(
                    scA1[:, :], scA1[:, :], AF.Ln,
                    bias=cst[:, CB_EPSR:CB_EPSR + 1], scale=1.0 / 32.0),
                    reads=("scA1", "cst"), writes=("scA1",))
                P.op("scalar", lambda s: s.activation(
                    scA1[:, :], scA1[:, :], AF.Exp, bias=CZ, scale=-0.5),
                    reads=("scA1",), writes=("scA1",))
                P.op("vector", lambda v, g_=g, q_=Q: v.scalar_tensor_tensor(
                    ks[:, g_ * T + q_ * QT:g_ * T + (q_ + 1) * QT],
                    bfA1[:, :], kg[:, 0:1], scA1[:, :],
                    op0=ALU.mult, op1=ALU.mult),
                    reads=("bfA1", "kg", "scA1"), writes=(f"ks{g}q{Q}",))
                yield
                kc = 4 * Q + g
                P.group("tensor",
                        [(lambda t, j_=g: t.matmul(
                            gbp[:, :], kvbf[:, j_ * 128:(j_ + 1) * 128],
                            wv[:, :], start=True, stop=True,
                            skip_group_check=True), [])],
                        reads=("kvbf", "wv0"), writes=("gb",))
                P.op("vector", lambda v, k_=kc: v.tensor_copy(
                    bass.AP(vtm.ap().tensor, k_ * 768,
                            [[NKC * 768, 128], [96, 8], [1, 32]]),
                    bass.AP(gbp.ap().tensor, 0,
                            [[512, 128], [64, 8], [1, 32]])),
                    reads=("gb",), writes=(f"vtm{kc}",))
                P.op("vector", lambda v, k_=kc: v.tensor_copy(
                    bass.AP(vtm.ap().tensor, k_ * 768 + 64,
                            [[NKC * 768, 128], [96, 8], [1, 32]]),
                    bass.AP(gbp.ap().tensor, 32,
                            [[512, 128], [64, 8], [1, 32]])),
                    reads=("gb",), writes=(f"vtm{kc}",))
                yield

        def stT(b, Q):
            par = Q % 2
            qb0 = par * 4 * QT
            qkey = f"qs{par}"
            NKCQ = 4 * (Q + 1)
            for g in range(4):
                wdeps_pv = P.wbegin(("pv",))
                wdeps_nm = P.wbegin(("nm",))
                last_pv = last_nm = first_pv = first_nm = None
                pv_groups, nm_groups = [], []
                for kc in range(NKCQ):
                    diag = kc >= 4 * Q
                    jd = kc - 4 * Q
                    qoff = 128 * jd if diag else 0
                    nq = QT - qoff
                    slot = kc % 2
                    ekey = f"eb{slot}"
                    sc_half = []
                    prev_mm = None
                    for hh in range(4):
                        d = [prev_mm]
                        if hh == 0:
                            d += P.wbegin(("sp0", "sp1", "sp2", "sp3"))
                        prev_mm = P.op(
                            "tensor",
                            lambda t, h=hh, g_=g, k_=kc, qo=qoff, n=nq:
                            t.matmul(
                                sp[:, 512 * h + qo:512 * h + qo + n],
                                ks[32 * h:32 * (h + 1),
                                   g_ * T + k_ * KB:
                                   g_ * T + (k_ + 1) * KB],
                                qs2[32 * h:32 * (h + 1),
                                    qb0 + g_ * QT + qo:
                                    qb0 + g_ * QT + qo + n],
                                start=True, stop=True,
                                tile_position=(32 * h, 0),
                                skip_group_check=True),
                            reads=(f"ks{g}q{kc // 4}", f"{qkey}_{g}"),
                            deps=d)
                        if hh == 1:
                            P.wend(prev_mm, ("sp0", "sp1"))
                            sc_half.append(prev_mm)
                        elif hh == 3:
                            P.wend(prev_mm, ("sp2", "sp3"))
                            sc_half.append(prev_mm)
                    e_half = []
                    for half in range(2):
                        e_half.append(P.op(
                            "scalar",
                            lambda s, sl=slot, qo=qoff, n=nq, hf=half:
                            s.activation(
                                ebuf[:, sl * 2048:(sl + 1) * 2048].rearrange(
                                    "p (h q) -> p h q",
                                    h=4)[:, 2 * hf:2 * hf + 2, qo:qo + n],
                                sp[:, :].rearrange(
                                    "p (h q) -> p h q",
                                    h=4)[:, 2 * hf:2 * hf + 2, qo:qo + n],
                                AF.Exp, bias=cst[:, CB_EXPB:CB_EXPB + 1],
                                scale=0.125),
                            reads=(f"sp{2 * half}", f"sp{2 * half + 1}",
                                   "cst"),
                            writes=(f"{ekey}_{half}",)))
                    if diag:
                        m_half = []
                        for half in range(2):
                            m_half.append(P.op(
                                "vector",
                                lambda v, sl=slot, qo=qoff, hf=half:
                                v.tensor_tensor(
                                    ebuf[:, sl * 2048:(sl + 1) * 2048]
                                    .rearrange("p (h q) -> p h q", h=4)
                                    [:, 2 * hf:2 * hf + 2, qo:qo + 128],
                                    ebuf[:, sl * 2048:(sl + 1) * 2048]
                                    .rearrange("p (h q) -> p h q", h=4)
                                    [:, 2 * hf:2 * hf + 2, qo:qo + 128],
                                    tri[:, :].rearrange(
                                        "p (r q) -> p r q", r=2),
                                    op=ALU.mult),
                                reads=(f"{ekey}_{half}", "tri"),
                                writes=(f"{ekey}_{half}",)))
                        e_half = m_half
                    fns = []
                    for jj in range(4):
                        vo = (kc * 768 + 96 * (2 * g + jj // 2)
                              + 32 * (jj % 2))
                        ba_ = pvp if jj < 2 else nmp
                        fns.append((
                            lambda t, j_=jj, vo_=vo, ba=ba_, qo=qoff, n=nq,
                            sl=slot, st=(kc == 0), sp_=(kc == NKCQ - 1):
                            t.matmul(
                                ba[64 * (j_ % 2):64 * (j_ % 2) + 64,
                                   qo:qo + n],
                                vtm[:, vo_:vo_ + 64],
                                ebuf[:, sl * 2048 + 512 * j_ + qo:
                                     sl * 2048 + 512 * j_ + qo + n],
                                start=st, stop=sp_,
                                tile_position=(0, 64 * (j_ % 2)),
                                skip_group_check=True),
                            [e_half[jj // 2]]))
                    g_pv = P.group(
                        "tensor", fns,
                        reads=(f"vtm{kc}", "vones"),
                        mreads=[(f"{ekey}_{jj // 2}",) for jj in range(4)],
                        deps0=((wdeps_pv + wdeps_nm) if kc == 0
                               else [first_pv]))
                    if kc == 0:
                        first_pv = g_pv
                    pv_groups.append(g_pv)
                    last_pv = g_pv
                    yield
                P.wend(last_pv, ("pv", "nm"))
                allg = tuple(pv_groups)
                for sc_, k_, bp_, bk_ in ((scT1, "scT1", pvp, "pv"),
                                          (scT0, "scT0", nmp, "nm")):
                    P.op("scalar", lambda s, d=sc_, p_=bp_: s.activation(
                        d[:, :], p_[:, :], AF.Square, bias=CZ),
                        reads=(bk_, "cst"), writes=(k_,), deps=allg)
                    P.op("scalar", lambda s, d=sc_: s.activation(
                        d[:, :], d[:, :], AF.Ln,
                        bias=cst[:, CB_TINY:CB_TINY + 1]),
                        reads=(k_, "cst"), writes=(k_,))
                    P.op("scalar", lambda s, d=sc_: s.activation(
                        d[:, :], d[:, :], AF.Exp, bias=CZ, scale=-0.5),
                        reads=(k_,), writes=(k_,))
                for jj in range(4):
                    ba_ = pvp if jj < 2 else nmp
                    bk = "pv" if jj < 2 else "nm"
                    sc_, sk = (scT1, "scT1") if jj < 2 else (scT0, "scT0")
                    vr = 64 * (jj % 2) + 32 * (jj % 2)
                    zr = 64 * (jj % 2) + 32 * (1 - jj % 2)
                    P.op("vector", lambda v, j_=jj, ba=ba_, sc2=sc_, g_=g,
                         v_=vr, z_=zr:
                         v.tensor_tensor(
                             outc[32 * j_:32 * j_ + 32,
                                  g_ * QT:(g_ + 1) * QT],
                             ba[v_:v_ + 32, :],
                             sc2[z_:z_ + 32, :],
                             op=ALU.mult),
                         reads=(bk, sk), writes=(f"outc{g}",), deps=allg)
                yield
            yield "wo"
            for dc in range(NCH):
                sl = dc % 2
                load(x1r[:, sl * 512:(sl + 1) * 512],
                     x0_d[b, Q][dc * 128:(dc + 1) * 128, :], f"x1r{sl}",
                     f"x1r{sl}")
                bank, bap = (("pv", pvp) if dc % 2 == 0 else ("nm", nmp))
                fns = [(lambda t, g_=gg, c=dc, ba=bap, st=(gg == 0),
                        sp_=(gg == 3): t.matmul(
                            ba[:, :],
                            wo[:, g_ * D + c * 128:g_ * D + (c + 1) * 128],
                            outc[:, g_ * QT:(g_ + 1) * QT],
                            start=st, stop=sp_, skip_group_check=True), [])
                       for gg in range(4)]
                P.group("tensor", fns, writes=(bank,),
                        mreads=[(f"wo{gg}", f"outc{gg}") for gg in range(4)])
                P.op("vector", lambda v, c=dc, ba=bap, s_=sl:
                     v.scalar_tensor_tensor(
                         ybuf[:, c * (QT + 1) + 1:c * (QT + 1) + 1 + QT],
                         ba[:, :], CZ,
                         x1r[:, s_ * 512:(s_ + 1) * 512],
                         op0=ALU.bypass, op1=ALU.add),
                     reads=(bank, "cst", f"x1r{sl}"), writes=(f"yb{dc}",))
                if dc % 2 == 1:
                    yield
            yield from layer_norm(True, "pv", pvp, "nm", nmp,
                                  scT0, "scT0", scT1, "scT1",
                                  bfT0, "bfT0", bfT1, "bfT1", None, None)

        def layer_norm(inplace, bka, bpa, bkb, bpb, s0, k0, s1, k1,
                       ba, kba, bb, kbb, dst, dkeys):
            wda = P.wbegin((bka,))
            wdb = P.wbegin((bkb,))
            la = lb = None
            for cc in range(NCH):
                base = cc * (QT + 1) + 1
                cba = ba if cc % 2 == 0 else bfT2
                cbk = kba if cc % 2 == 0 else "bfT2"
                P.op("vector", lambda v, b_=base, dd=cba: v.tensor_copy(
                    dd[:, :], ybuf[:, b_:b_ + QT]),
                    reads=(f"yb{cc}",), writes=(cbk,))
                P.op("gpsimd", lambda v, dd=cba: v.tensor_tensor(
                    bb[:, :], dd[:, :], dd[:, :], op=ALU.mult),
                    reads=(cbk,), writes=(kbb,))
                la = P.op("tensor",
                          lambda t, st=(cc == 0), sp_=(cc == NCH - 1),
                          dd=cba:
                          t.matmul(bpa[:, :], onf[:, :], dd[:, :],
                                   start=st, stop=sp_,
                                   skip_group_check=True),
                          reads=("onf", cbk),
                          deps=(wda if cc == 0 else [la]))
                lb = P.op("tensor",
                          lambda t, st=(cc == 0), sp_=(cc == NCH - 1):
                          t.matmul(bpb[:, :], onf[:, :], bb[:, :],
                                   start=st, stop=sp_,
                                   skip_group_check=True),
                          reads=("onf", kbb),
                          deps=(wdb if cc == 0 else [lb]))
            P.wend(la, (bka,))
            P.wend(lb, (bkb,))
            P.op("scalar", lambda s: s.activation(
                s0[:, :], bpa[:, :], AF.Square, bias=CZ, scale=1.0 / 1024.0),
                reads=(bka, "cst"), writes=(k0,))
            P.op("vector", lambda v: v.scalar_tensor_tensor(
                s1[:, :], bpb[:, :], cst[:, CB_R1024:CB_R1024 + 1],
                s0[:, :], op0=ALU.mult, op1=ALU.subtract),
                reads=(bkb, "cst", k0), writes=(k1,))
            P.op("scalar", lambda s: s.activation(
                s1[:, :], s1[:, :], AF.Ln,
                bias=cst[:, CB_EPSL:CB_EPSL + 1]),
                reads=(k1, "cst"), writes=(k1,))
            P.op("scalar", lambda s: s.activation(
                s1[:, :], s1[:, :], AF.Exp, bias=CZ, scale=-0.5),
                reads=(k1,), writes=(k1,))
            P.op("vector", lambda v: v.scalar_tensor_tensor(
                s0[:, :], bpa[:, :], cst[:, CB_RN1024:CB_RN1024 + 1],
                s1[:, :], op0=ALU.mult, op1=ALU.mult),
                reads=(bka, "cst", k1), writes=(k0,))
            yield
            for cc in range(NCH):
                base = cc * (QT + 1) + 1
                if inplace:
                    da, dk = ybuf[:, base:base + QT], f"yb{cc}"
                    P.op("vector", lambda v, b_=base, dd=da: v.tensor_tensor(
                        dd, ybuf[:, b_:b_ + QT], s1[:, :], op=ALU.mult),
                        reads=(dk, k1), writes=(dk,))
                    P.op("vector", lambda v, dd=da: v.tensor_tensor(
                        dd, dd, s0[:, :], op=ALU.add),
                        reads=(dk, k0), writes=(dk,))
                else:
                    da, dk = ybuf[:, base:base + QT], f"yb{cc}"
                    P.op("gpsimd", lambda v, dd=da: v.tensor_tensor(
                        dd, dd, s1[:, :], op=ALU.mult),
                        reads=(dk, k1), writes=(dk,))
                    P.op("gpsimd", lambda v, dd=da: v.tensor_tensor(
                        dd, dd, s0[:, :], op=ALU.add),
                        reads=(dk, k0), writes=(dk,))
                    so = P.op("sync",
                              lambda s, b_=layer_norm.b, q_=layer_norm.q,
                              c=cc, dd=da: s.dma_start(
                                  out_d[b_, q_][c * 128:(c + 1) * 128, :],
                                  dd),
                              reads=(dk,), dma=f"or{cc % 2}")
                    store_ops.append(so)
                if cc % 4 == 3:
                    yield

        def stF(b, Q):
            wrtr = tuple(f"wrt{c}" for c in range(NCH))
            wktr = tuple(f"wkt{c}" for c in range(NCH))
            wvtr = tuple(f"wvt{c}" for c in range(NCH))
            xr_ops, xk_ops, kact_ops = [], [], []
            for cc in range(NCH):
                base = cc * (QT + 1)
                dsb = bfF0 if cc % 2 == 0 else bfF2
                dsk = "bfF0" if cc % 2 == 0 else "bfF2"
                P.op("gpsimd", lambda v, b_=base, dd=dsb: v.tensor_tensor(
                    dd[:, :], ybuf[:, b_ + 1:b_ + 1 + QT],
                    ybuf[:, b_:b_ + QT], op=ALU.subtract),
                    reads=(f"yb{cc}",), writes=(dsk,))
                xr_ops.append(P.op("vector", lambda v, c=cc, b_=base,
                     dd=dsb: v.scalar_tensor_tensor(
                         xr[:, c * QT:(c + 1) * QT], dd[:, :],
                         mur[:, c:c + 1], ybuf[:, b_:b_ + QT],
                         op0=ALU.mult, op1=ALU.add),
                     reads=(dsk, "mur", f"yb{cc}"), writes=(f"xr{cc}",)))
                xk_ops.append(P.op("vector", lambda v, c=cc, b_=base,
                     dd=dsb: v.scalar_tensor_tensor(
                         xk[:, c * QT:(c + 1) * QT], dd[:, :],
                         muk[:, c:c + 1], ybuf[:, b_:b_ + QT],
                         op0=ALU.mult, op1=ALU.add),
                     reads=(dsk, "muk", f"yb{cc}"), writes=(f"xk{cc}",)))
                if cc % 4 == 3:
                    yield
            if Q == n_quarters - 1:
                for cc in range(NCH):
                    P.op("vector", lambda v, c=cc: v.memset(
                        ybuf[:, c * (QT + 1):c * (QT + 1) + 1], 0.0),
                        writes=(f"yb{cc}",))
            else:
                for cc in range(NCH):
                    P.op("vector", lambda v, c=cc: v.tensor_copy(
                        ybuf[:, c * (QT + 1):c * (QT + 1) + 1],
                        ybuf[:, c * (QT + 1) + QT:c * (QT + 1) + QT + 1]),
                        reads=(f"yb{cc}",), writes=(f"yb{cc}",))
            yield
            for dc in range(NCH):
                fns = [(lambda t, c=cc_, d_=dc, st=(cc_ == 0),
                        sp_=(cc_ == NCH - 1): t.matmul(
                            gap[:, :],
                            wrt[:, c * D + d_ * 128:c * D + (d_ + 1) * 128],
                            xr[:, c * QT:(c + 1) * QT],
                            start=st, stop=sp_, skip_group_check=True),
                        [xr_ops[cc_]])
                       for cc_ in range(NCH)]
                P.group("tensor", fns, writes=("ga",),
                        mreads=[(f"wrt{c}", f"xr{c}") for c in range(NCH)])
                P.op("vector", lambda v, c=dc: v.tensor_copy(
                    esig[:, c * QT:(c + 1) * QT], gap[:, :]),
                    reads=("ga",), writes=(f"esig{dc}",))
                if dc % 2 == 1:
                    P.op("scalar", lambda s, c=dc: s.activation(
                        esig[:, (c - 1) * QT:(c + 1) * QT],
                        esig[:, (c - 1) * QT:(c + 1) * QT], AF.Exp,
                        bias=CZ, scale=-1.0),
                        reads=(f"esig{dc - 1}", f"esig{dc}", "cst"),
                        writes=(f"esig{dc - 1}", f"esig{dc}"))
                yield
                fns = [(lambda t, c=cc_, d_=dc, st=(cc_ == 0),
                        sp_=(cc_ == NCH - 1): t.matmul(
                            gbp[:, :],
                            wkt[:, c * D + d_ * 128:c * D + (d_ + 1) * 128],
                            xk[:, c * QT:(c + 1) * QT],
                            start=st, stop=sp_, skip_group_check=True),
                        [xk_ops[cc_]])
                       for cc_ in range(NCH)]
                P.group("tensor", fns, writes=("gb",),
                        mreads=[(f"wkt{c}", f"xk{c}") for c in range(NCH)])
                P.op("vector", lambda v: v.tensor_scalar_max(
                    bfF1[:, :], gbp[:, :], CZ),
                    reads=("gb", "cst"), writes=("bfF1",))
                kact_ops.append(P.op("vector", lambda v, c=dc: v.tensor_tensor(
                    kact[:, c * QT:(c + 1) * QT], bfF1[:, :],
                    bfF1[:, :], op=ALU.mult),
                    reads=("bfF1",), writes=(f"kact{dc}",)))
                yield
            for dc2 in range(NCH):
                bank, bap = (("ga", gap) if dc2 % 2 == 0 else ("gb", gbp))
                fns = [(lambda t, c=cc_, d_=dc2, ba=bap, st=(cc_ == 0),
                        sp_=(cc_ == NCH - 1): t.matmul(
                            ba[:, :],
                            wvt[:, c * D + d_ * 128:c * D + (d_ + 1) * 128],
                            kact[:, c * QT:(c + 1) * QT],
                            start=st, stop=sp_, skip_group_check=True),
                        [kact_ops[cc_]])
                       for cc_ in range(NCH)]
                P.group("tensor", fns, writes=(bank,),
                        mreads=[(f"wvt{c}", f"kact{c}") for c in range(NCH)])
                P.op("vector", lambda v, c=dc2: v.tensor_scalar_add(
                    scF0[:, :], esig[:, c * QT:(c + 1) * QT],
                    cst[:, CB_ONE:CB_ONE + 1]),
                    reads=(f"esig{dc2}", "cst"), writes=("scF0",))
                P.op("scalar", lambda s: s.activation(
                    scF0[:, :], scF0[:, :], AF.Ln, bias=CZ),
                    reads=("scF0", "cst"), writes=("scF0",))
                P.op("scalar", lambda s: s.activation(
                    scF0[:, :], scF0[:, :], AF.Exp, bias=CZ, scale=-1.0),
                    reads=("scF0",), writes=("scF0",))
                P.op("vector", lambda v, ba=bap: v.scalar_tensor_tensor(
                    scF0[:, :], ba[:, :], CZ, scF0[:, :],
                    op0=ALU.bypass, op1=ALU.mult),
                    reads=(bank, "cst", "scF0"), writes=("scF0",))
                P.op("vector", lambda v, c=dc2: v.tensor_tensor(
                    ybuf[:, c * (QT + 1) + 1:c * (QT + 1) + 1 + QT],
                    ybuf[:, c * (QT + 1) + 1:c * (QT + 1) + 1 + QT],
                    scF0[:, :], op=ALU.add),
                    reads=(f"yb{dc2}", "scF0"), writes=(f"yb{dc2}",))
                yield
            layer_norm.b, layer_norm.q = b, Q
            yield from layer_norm(False, "ga", gap, "gb", gbp,
                                  scF0, "scF0", scF1, "scF1",
                                  bfF0, "bfF0", bfF1, "bfF1",
                                  None, None)

        genA = [stA(b, q) for b in range(b_core) for q in range(n_quarters)]
        genT = [stT(b, q) for b in range(b_core) for q in range(n_quarters)]
        genF = [stF(b, q) for b in range(b_core) for q in range(n_quarters)]
        NS = b_core * n_quarters

        def n_units_T(i):
            return 4 * (4 * ((i % n_quarters) + 1) + 1) + 8 + 3
        for s in range(NS + 2):
            drive(genF[s - 2] if 0 <= s - 2 < NS else None,
                  genT[s - 1] if 0 <= s - 1 < NS else None,
                  genA[s] if s < NS else None,
                  nf=34, nt=n_units_T(s - 1) if 1 <= s <= NS else 1, na=12)

        P.op("sync", None, deps=tuple(store_ops))

        dma_c_ops = [o for o in P.all if o.dma == "dma_c"]
        for o in P.all:
            if o.dma is None and any(d.dma == "dma_c" for d in o.deps):
                have = set(map(id, o.deps))
                o.deps = tuple(o.deps) + tuple(
                    x for x in dma_c_ops if id(x) not in have)
        P.schedule()
        P.finalize()
        nc._prog = P
        for o in P.all:
            if o.dma == "dma_c":
                o.count = 16 * 8
        for e in ENGINES:
            for o in P.ops[e]:
                o.waits = [(c, (16 * 8 if c == "dma_c" else v))
                           for (c, v) in o.waits]
        blk = EC(nc.Block())

        def mk(eng_name):
            def emitter(eng):
                for o in P.ops[eng_name]:
                    for (pp, c) in o.waits:
                        eng.wait_ge(sems[pp], c)
                    if o.fn is None:
                        continue
                    instr = o.fn(eng)
                    if o.inc:
                        instr.then_inc(sems[o.chan], 16 if o.dma else 1)
            return emitter

        blk.sync(mk("sync"))
        blk.tensor(mk("tensor"))
        blk.vector(mk("vector"))
        blk.scalar(mk("scalar"))
        blk.gpsimd(mk("gpsimd"))

    return nc



def _prep_inputs(L0_attn_x, L0_attn_Wq, L0_attn_Wkv, L0_attn_Wk, L0_attn_Wv,
                 L0_attn_Wo, L0_attn_q_g, L0_attn_k_g, L0_ffn_W_r,
                 L0_ffn_W_k, L0_ffn_W_v, L0_ffn_mu_r, L0_ffn_mu_k):
    n_quarters = 4
    wq = np.asarray(L0_attn_Wq, dtype=np.float32)
    wq_r = wq.reshape(D, H, HD)
    wq_perm = np.concatenate([wq_r[:, :, :ROPE].reshape(D, H * ROPE),
                              wq_r[:, :, ROPE:].reshape(D, H * ROPE)], axis=1)
    wo = np.asarray(L0_attn_Wo, dtype=np.float32)
    wo_sub = np.ascontiguousarray(
        wo.reshape(H, HD, D)[:, :ROPE, :].reshape(H * ROPE, D))
    shared = {
        "wq": np.ascontiguousarray(wq_perm),
        "wkv": np.ascontiguousarray(np.asarray(L0_attn_Wkv, np.float32)),
        "wk": np.ascontiguousarray(np.asarray(L0_attn_Wk, np.float32)),
        "wv": np.ascontiguousarray(np.asarray(L0_attn_Wv, np.float32)),
        "wo": wo_sub,
        "wrt": np.ascontiguousarray(np.asarray(L0_ffn_W_r, np.float32).T),
        "wkt": np.ascontiguousarray(np.asarray(L0_ffn_W_k, np.float32).T),
        "wvt": np.ascontiguousarray(np.asarray(L0_ffn_W_v, np.float32).T),
        "mur": np.ascontiguousarray(
            np.asarray(L0_ffn_mu_r, np.float32).reshape(NCH, 128).T),
        "muk": np.ascontiguousarray(
            np.asarray(L0_ffn_mu_k, np.float32).reshape(NCH, 128).T),
        "qg": np.ascontiguousarray(
            np.tile(np.asarray(L0_attn_q_g, np.float32)[:ROPE], 4)
            .reshape(128, 1)),
        "kg": np.ascontiguousarray(
            np.tile(np.asarray(L0_attn_k_g, np.float32), 4).reshape(128, 1)),
        "cst": np.tile(np.array([CONST_VALS], np.float32), (128, 1)),
        "tri": np.ascontiguousarray(np.tile(
            (np.arange(128)[None, :] >= np.arange(128)[:, None])
            .astype(bfloat16), (1, 2))),
        "obd": np.ascontiguousarray(
            (np.arange(128)[:, None] // 32
             == np.arange(128)[None, :] // 32).astype(bfloat16)),
        "onf": np.ones((128, 128), dtype=bfloat16),
    }
    x = np.asarray(L0_attn_x, np.float32)
    bsz, t, d = x.shape
    b_core = bsz // N_CORES
    in_maps = []
    for c in range(N_CORES):
        xs = x[c * b_core:(c + 1) * b_core]
        xq = xs.reshape(b_core, n_quarters, QT, d)
        xq = np.ascontiguousarray(xq.transpose(0, 1, 3, 2))
        in_maps.append(dict(shared, x0=xq))
    return in_maps, b_core, n_quarters


def kernel(**inputs):
    in_maps, b_core, n_quarters = _prep_inputs(**inputs)
    nc = build_program(n_quarters=n_quarters, b_core=b_core)
    res = run_bass_kernel_spmd(nc, in_maps, core_ids=list(range(N_CORES)))
    outs = []
    for c in range(N_CORES):
        o = np.asarray(res.results[c]["out"])
        o = o.transpose(0, 1, 3, 2).reshape(b_core, n_quarters * QT, D)
        outs.append(o)
    return np.ascontiguousarray(np.concatenate(outs, axis=0))


# version 25
# speedup vs baseline: 1.0320x; 1.0017x over previous
import contextlib

import numpy as np
import ml_dtypes

import concourse.bass as bass
import concourse.mybir as mybir
from concourse.bass_utils import run_bass_kernel_spmd

F32 = mybir.dt.float32
BF16 = mybir.dt.bfloat16
AF = mybir.ActivationFunctionType
ALU = mybir.AluOpType
bfloat16 = ml_dtypes.bfloat16

D = 1024
H = 16
HD = 64
ROPE = 32
LAT = 128
NCH = 8
QT = 512
KB = 128
N_CORES = 8

(CB_EXPB, CB_EPSR, CB_EPSL, CB_ZERO, CB_ONE, CB_R1024, CB_RN1024,
 CB_TINY) = range(8)
CONST_VALS = [-6.0, 1e-6, 1e-5, 0.0, 1.0, 1.0 / 1024.0, -1.0 / 1024.0,
              1e-30]

ENGINES = ("sync", "tensor", "vector", "scalar", "gpsimd")


class Op:
    __slots__ = ("eng", "fn", "deps", "inc", "count", "waits", "dma",
                 "chan", "rw")

    def __init__(self, eng, fn, deps, dma):
        self.rw = None
        self.eng = eng
        self.fn = fn
        self.deps = deps
        self.dma = dma
        self.chan = None
        self.inc = False
        self.count = None
        self.waits = []


class Prog:

    def __init__(self):
        self.ops = {e: [] for e in ENGINES}
        self.all = []
        self.bufs = {}

    def _pre(self, reads, writes):
        deps = []
        for k in reads:
            st = self.bufs.get(k)
            if st and st["w"] is not None:
                deps.append(st["w"])
        for k in writes:
            st = self.bufs.get(k)
            if st:
                if st["r"]:
                    deps.extend(st["r"])
                elif st["w"] is not None:
                    deps.append(st["w"])
        return deps

    def _post(self, op, reads, writes):
        for k in reads:
            self.bufs.setdefault(k, {"w": None, "r": []})["r"].append(op)
        for k in writes:
            self.bufs[k] = {"w": op, "r": []}

    def op(self, eng, fn, reads=(), writes=(), deps=(), dma=None):
        d = self._pre(reads, writes) + [x for x in deps if x is not None]
        o = Op(eng, fn, d, dma)
        o.rw = (tuple(reads), tuple(writes))
        self.ops[eng].append(o)
        self.all.append(o)
        self._post(o, reads, writes)
        return o

    def group(self, eng, fns, reads=(), writes=(), deps0=(), mreads=None):
        d0 = self._pre(reads, writes) + [x for x in deps0 if x is not None]
        last = None
        regs = list(reads)
        for i, (fn, extra) in enumerate(fns):
            dd = (d0 if i == 0 else []) + [x for x in extra if x is not None]
            mr = ()
            if mreads is not None:
                mr = tuple(mreads[i])
                regs.extend(mr)
                for k in mr:
                    st = self.bufs.get(k)
                    if st and st["w"] is not None:
                        dd.append(st["w"])
            if last is not None:
                dd = dd + [last]
            nxt = Op(eng, fn, dd, False)
            nxt.rw = (tuple(reads) + mr, tuple(writes))
            self.ops[eng].append(nxt)
            self.all.append(nxt)
            last = nxt
        self._post(last, tuple(regs), writes)
        return last

    def wbegin(self, keys):
        return self._pre((), keys)

    def wend(self, op, keys):
        for k in keys:
            self.bufs[k] = {"w": op, "r": []}

    def est_cost(self, o):
        if o.dma:
            return 790.0
        if o.eng == "tensor":
            rw = o.rw[0] + o.rw[1]
            if any(k.startswith(("eb", "vtm", "ks")) for k in rw) or \
               "onf" in o.rw[0][:1]:
                return 110.0
            return 240.0
        if o.eng == "scalar":
            return 1000.0 if any(k.startswith("eb") for k in o.rw[1]) \
                else 650.0
        if o.eng == "gpsimd":
            return 1000.0
        return 620.0

    def schedule(self):
        import heapq
        PROP = 200.0
        DMA_LAT = 600.0
        succs = {id(o): [] for o in self.all}
        indeg = {id(o): 0 for o in self.all}
        for o in self.all:
            for d in o.deps:
                succs[id(d)].append(o)
                indeg[id(o)] += 1
        prio = {}
        for o in reversed(self.all):
            c = self.est_cost(o)
            best = 0.0
            for s in succs[id(o)]:
                best = max(best, prio[id(s)])
            prio[id(o)] = c + best

        avail = {e: 0.0 for e in ENGINES}
        done_t = {}
        ready = {e: [] for e in ENGINES}
        seq = {e: 0 for e in ENGINES}
        emit_idx = {id(o): i for i, o in enumerate(self.all)}
        for o in self.all:
            if indeg[id(o)] == 0:
                heapq.heappush(ready[o.eng],
                               (-prio[id(o)], emit_idx[id(o)], id(o), o))
        new_ops = {e: [] for e in ENGINES}
        n_done = 0
        pending = []
        def start_time(o, e):
            t0 = avail[e]
            for d in o.deps:
                t0 = max(t0, done_t[id(d)] +
                         (0.0 if d.eng == o.eng and not d.dma else PROP))
            return t0
        while n_done < len(self.all):
            best = None
            for e in ENGINES:
                if not ready[e]:
                    continue
                cand = heapq.nsmallest(8, ready[e])
                eb = None
                for item in cand:
                    o = item[3]
                    t0 = start_time(o, e)
                    if eb is None or t0 < eb[0] - 200.0 or \
                       (abs(t0 - eb[0]) <= 200.0 and
                        prio[id(o)] > prio[id(eb[1])]):
                        eb = (t0, o, item)
                if best is None or eb[0] < best[0] or \
                   (eb[0] == best[0] and prio[id(eb[1])] > prio[id(best[1])]):
                    best = (eb[0], eb[1], e, eb[2])
            t0, o, e, item = best
            ready[e].remove(item)
            heapq.heapify(ready[e])
            c = self.est_cost(o)
            if o.dma:
                avail[e] = t0 + c
                fin = t0 + c + DMA_LAT
            else:
                avail[e] = t0 + c
                fin = t0 + c
            done_t[id(o)] = fin
            new_ops[e].append(o)
            n_done += 1
            for s in succs[id(o)]:
                indeg[id(s)] -= 1
                if indeg[id(s)] == 0:
                    heapq.heappush(ready[s.eng],
                                   (-prio[id(s)], emit_idx[id(s)], id(s), s))
        self.ops = new_ops

    def finalize(self):
        for o in self.all:
            for d in o.deps:
                if d.eng != o.eng or d.dma:
                    d.inc = True
        cnt = {}
        for e in ENGINES:
            for o in self.ops[e]:
                o.chan = o.dma if o.dma else e
                if o.dma:
                    o.inc = True
                if o.inc:
                    cnt[o.chan] = cnt.get(o.chan, 0) + (16 if o.dma else 1)
                    o.count = cnt[o.chan]
        hwm = {e: {} for e in ENGINES}
        for e in ENGINES:
            for o in self.ops[e]:
                for d in o.deps:
                    if d.chan == e and not d.dma:
                        continue
                    if d.count > hwm[e].get(d.chan, 0):
                        o.waits.append((d.chan, d.count))
                        hwm[e][d.chan] = d.count


def drive(gf, gt, ga, nf=1, nt=1, na=1):
    rem = {}
    for g, n in ((gf, nf), (gt, nt), (ga, na)):
        if g is not None:
            rem[g] = max(1, n)
    t_blocked = False
    while rem:
        order = sorted(rem, key=lambda g: -rem[g])
        g = order[0]
        if g is gt and t_blocked:
            others = [x for x in order[1:]]
            if gf in rem:
                g = others[0] if others else None
                if g is None:
                    continue
            else:
                t_blocked = False
        try:
            v = next(g)
            rem[g] = max(1, rem[g] - 1)
            if g is gt and v == "wo" and gf in rem:
                t_blocked = True
        except StopIteration:
            del rem[g]


def build_program(n_quarters=4, b_core=2):
    T = n_quarters * QT
    NKC = T // KB

    nc = bass.Bass("TRN2", target_bir_lowering=False,
                   dynamic_dma_scratch_size=512,
                   detect_race_conditions=False)

    x0_d = nc.dram_tensor("x0", [b_core, n_quarters, D, QT], F32,
                          kind="ExternalInput").ap()
    out_d = nc.dram_tensor("out", [b_core, n_quarters, D, QT], F32,
                           kind="ExternalOutput").ap()
    w_d = {}
    for nm, shp in [("wq", [D, D]), ("wkv", [D, LAT]),
                    ("wk", [LAT, H * ROPE]), ("wv", [LAT, H * ROPE]),
                    ("wo", [H * ROPE, D]), ("wrt", [D, D]),
                    ("wkt", [D, D]), ("wvt", [D, D])]:
        w_d[nm] = nc.dram_tensor(nm, shp, F32, kind="ExternalInput").ap()
    mur_d = nc.dram_tensor("mur", [128, NCH], F32, kind="ExternalInput").ap()
    muk_d = nc.dram_tensor("muk", [128, NCH], F32, kind="ExternalInput").ap()
    qg_d = nc.dram_tensor("qg", [128, 1], F32, kind="ExternalInput").ap()
    kg_d = nc.dram_tensor("kg", [128, 1], F32, kind="ExternalInput").ap()
    cst_d = nc.dram_tensor("cst", [128, len(CONST_VALS)], F32,
                           kind="ExternalInput").ap()
    tri_d = nc.dram_tensor("tri", [128, 256], BF16, kind="ExternalInput").ap()
    obd_d = nc.dram_tensor("obd", [128, 128], BF16, kind="ExternalInput").ap()
    onf_d = nc.dram_tensor("onf", [128, 128], BF16, kind="ExternalInput").ap()

    with contextlib.ExitStack() as es:
        EC = es.enter_context

        wq = EC(nc.sbuf_tensor("wqs", [128, NCH * D], BF16))
        wkv = EC(nc.sbuf_tensor("wkvs", [128, NCH * LAT], BF16))
        wk = EC(nc.sbuf_tensor("wks", [128, H * ROPE], BF16))
        wv = EC(nc.sbuf_tensor("wvs", [128, H * ROPE], BF16))
        wo = EC(nc.sbuf_tensor("wos", [128, 4 * D], BF16))
        wrt = EC(nc.sbuf_tensor("wrts", [128, NCH * D], BF16))
        wkt = EC(nc.sbuf_tensor("wkts", [128, NCH * D], BF16))
        wvt = EC(nc.sbuf_tensor("wvts", [128, NCH * D], BF16))
        mur = EC(nc.sbuf_tensor("murs", [128, NCH], F32))
        muk = EC(nc.sbuf_tensor("muks", [128, NCH], F32))
        qg = EC(nc.sbuf_tensor("qgs", [128, 1], F32))
        kg = EC(nc.sbuf_tensor("kgs", [128, 1], F32))
        cst = EC(nc.sbuf_tensor("csts", [128, len(CONST_VALS)], F32))
        tri = EC(nc.sbuf_tensor("tris", [128, 256], BF16))
        obd = EC(nc.sbuf_tensor("obds", [128, 128], BF16))
        onf = EC(nc.sbuf_tensor("onfs", [128, 128], BF16))
        ybuf = EC(nc.sbuf_tensor("ybuf", [128, NCH * (QT + 1)], F32))
        x0bf = EC(nc.sbuf_tensor("x0bf", [128, NCH * QT], BF16))
        xr = EC(nc.sbuf_tensor("xrb", [128, NCH * QT], BF16))
        xk = EC(nc.sbuf_tensor("xkb", [128, NCH * QT], BF16))
        qs2 = EC(nc.sbuf_tensor("qs2", [128, 2 * 4 * QT], BF16))
        ks = EC(nc.sbuf_tensor("ksb", [128, 4 * T], BF16))
        kvbf = EC(nc.sbuf_tensor("kvbf", [128, QT], BF16))
        ebuf = EC(nc.sbuf_tensor("ebuf", [128, 2 * 2048], BF16))
        outc = EC(nc.sbuf_tensor("outcb", [128, 4 * QT], BF16))
        esig = EC(nc.sbuf_tensor("esig", [128, NCH * QT], BF16))
        kact = EC(nc.sbuf_tensor("kactb", [128, NCH * QT], BF16))
        x0r = EC(nc.sbuf_tensor("x0r", [128, 2 * 512], F32))
        x1r = EC(nc.sbuf_tensor("x1r", [128, 2 * 512], F32))
        scA1 = EC(nc.sbuf_tensor("scA1", [128, 512], F32))
        scT0 = EC(nc.sbuf_tensor("scT0", [128, 512], F32))
        scT1 = EC(nc.sbuf_tensor("scT1", [128, 512], F32))
        scF0 = EC(nc.sbuf_tensor("scF0", [128, 512], F32))
        scF1 = EC(nc.sbuf_tensor("scF1", [128, 512], F32))
        bfA0 = EC(nc.sbuf_tensor("bfA0", [128, 512], BF16))
        bfA1 = EC(nc.sbuf_tensor("bfA1", [128, 512], BF16))
        bfA2 = EC(nc.sbuf_tensor("bfA2", [128, 512], BF16))
        bfT0 = EC(nc.sbuf_tensor("bfT0", [128, 512], BF16))
        bfT1 = EC(nc.sbuf_tensor("bfT1", [128, 512], BF16))
        bfF0 = EC(nc.sbuf_tensor("bfF0", [128, 512], BF16))
        bfF1 = EC(nc.sbuf_tensor("bfF1", [128, 512], BF16))
        bfF2 = EC(nc.sbuf_tensor("bfF2", [128, 512], BF16))
        bfT2 = EC(nc.sbuf_tensor("bfT2", [128, 512], BF16))
        vtm = EC(nc.sbuf_tensor("vtm", [128, NKC * 768], BF16))

        sp = EC(nc.psum_tensor("spp", [128, 2048], F32))
        pvp = EC(nc.psum_tensor("pvp", [128, 512], F32))
        nmp = EC(nc.psum_tensor("nmp", [128, 512], F32))
        gap = EC(nc.psum_tensor("gap", [128, 512], F32))
        gbp = EC(nc.psum_tensor("gbp", [128, 512], F32))

        DMA_CHANS = ("dma_c", "dma_w0", "dma_w1", "x0r0", "x0r1",
                     "x1r0", "x1r1", "or0", "or1")
        sems = {e: EC(nc.semaphore("sem_" + e))
                for e in list(ENGINES) + list(DMA_CHANS)}
        P = Prog()

        CZ = cst[:, CB_ZERO:CB_ZERO + 1]

        def load(dst_ap, src_ap, wkey, chan):
            return P.op("sync",
                        lambda s, dd=dst_ap, rr=src_ap: s.dma_start(dd, rr),
                        writes=(wkey,), dma=chan)

        load(cst[:, :], cst_d, "cst", "dma_c")
        load(mur[:, :], mur_d, "mur", "dma_c")
        load(muk[:, :], muk_d, "muk", "dma_c")
        load(qg[:, :], qg_d, "qg", "dma_c")
        load(kg[:, :], kg_d, "kg", "dma_c")
        load(tri[:, :], tri_d, "tri", "dma_c")
        load(obd[:, :], obd_d, "obd", "dma_c")
        load(onf[:, :], onf_d, "onf", "dma_c")

        wslot = [0]

        def wload(name, n_chunks, cols, dst, dst_cols):
            for ch in range(n_chunks):
                prev = None
                for p0 in range(0, cols, 256):
                    pw = min(256, cols - p0)
                    sl = wslot[0] % 2
                    wslot[0] += 1
                    src = w_d[name][ch * 128:(ch + 1) * 128, p0:p0 + pw]
                    P.op("sync",
                         lambda s, sr=src, w=pw, s_=sl: s.dma_start(
                             scF1[:, s_ * 256:s_ * 256 + w], sr),
                         writes=(f"stage{sl}",), dma=f"dma_w{sl}")
                    last = p0 + pw >= cols
                    prev = P.op(
                        "vector",
                        lambda v, c=ch, w=pw, q0=p0, dd=dst, dc=dst_cols,
                        s_=sl:
                        v.tensor_copy(dd[:, c * dc + q0:c * dc + q0 + w],
                                      scF1[:, s_ * 256:s_ * 256 + w]),
                        reads=(f"stage{sl}",),
                        writes=(f"{name}{ch}",) if last else (),
                        deps=(prev,))

        wload("wq", NCH, D, wq, D)
        _stage_fence_pending = True
        wload("wkv", NCH, LAT, wkv, LAT)
        wload("wk", 1, H * ROPE, wk, H * ROPE)
        wload("wv", 1, H * ROPE, wv, H * ROPE)
        wload("wo", 4, D, wo, D)
        wload("wrt", NCH, D, wrt, D)
        wload("wkt", NCH, D, wkt, D)
        wload("wvt", NCH, D, wvt, D)
        P.op("vector", lambda v: v.memset(scF1[0:1, 0:1], 0.0),
             writes=("stage0", "stage1", "scF1"))

        P.op("vector", lambda v: v.tensor_copy(
            bass.AP(vtm.ap().tensor, 32,
                    [[NKC * 768, 128], [96, NKC * 8], [1, 32]]),
            bass.AP(onf.ap().tensor, 0,
                    [[128, 128], [0, NKC * 8], [1, 32]])),
            reads=("onf",), writes=("vones",))

        for cc in range(NCH):
            P.op("vector", lambda v, c=cc: v.memset(
                ybuf[:, c * (QT + 1):c * (QT + 1) + 1], 0.0),
                writes=(f"yb{cc}",))

        store_ops = []


        def stA(b, Q):
            par = Q % 2
            qb0 = par * 4 * QT
            qkey = f"qs{par}"
            cast_ops = []
            for w in range(4):
                for sl in range(2):
                    cc = 2 * w + sl
                    load(x0r[:, sl * 512:(sl + 1) * 512],
                         x0_d[b, Q][cc * 128:(cc + 1) * 128, :], f"x0r{sl}",
                         f"x0r{sl}")
                cw = P.op(
                    "vector", lambda v, w_=w: v.tensor_copy(
                        x0bf[:, w_ * 2 * QT:(w_ + 1) * 2 * QT],
                        x0r[:, :]),
                    reads=("x0r0", "x0r1"),
                    writes=tuple(f"x0bf{2 * w + s}" for s in range(2)))
                cast_ops.extend([cw] * 2)
                if w % 2 == 1:
                    yield
            wqr = tuple(f"wq{c}" for c in range(NCH))
            x0keys = tuple(f"x0bf{c}" for c in range(NCH))
            for g in range(4):
                fns = [(lambda t, c=cc_, o=4 + g, st=(cc_ == 0),
                        sp_=(cc_ == NCH - 1): t.matmul(
                            gap[:, :],
                            wq[:, c * D + o * 128:c * D + (o + 1) * 128],
                            x0bf[:, c * QT:(c + 1) * QT],
                            start=st, stop=sp_, skip_group_check=True),
                        [cast_ops[cc_]])
                       for cc_ in range(NCH)]
                P.group("tensor", fns, writes=("ga",),
                        mreads=[(f"wq{c}", f"x0bf{c}") for c in range(NCH)])
                P.op("vector", lambda v: v.tensor_copy(bfA0[:, :],
                                                       gap[:, :]),
                    reads=("ga",), writes=("bfA0",))
                P.op("vector", lambda v: v.tensor_tensor(
                    bfA2[:, :], bfA0[:, :], bfA0[:, :], op=ALU.mult),
                    reads=("bfA0",), writes=("bfA2",))
                yield
                fns = [(lambda t, c=cc_, o=g, st=(cc_ == 0),
                        sp_=(cc_ == NCH - 1): t.matmul(
                            gbp[:, :],
                            wq[:, c * D + o * 128:c * D + (o + 1) * 128],
                            x0bf[:, c * QT:(c + 1) * QT],
                            start=st, stop=sp_, skip_group_check=True),
                        [cast_ops[cc_]])
                       for cc_ in range(NCH)]
                P.group("tensor", fns, writes=("gb",),
                        mreads=[(f"wq{c}", f"x0bf{c}") for c in range(NCH)])
                P.op("vector", lambda v: v.tensor_copy(bfA1[:, :],
                                                       gbp[:, :]),
                    reads=("gb",), writes=("bfA1",))
                wdg = P.wbegin(("ga",))
                m1 = P.op("tensor", lambda t: t.matmul(
                    gap[:, :], obd[:, :], bfA2[:, :], start=True,
                    stop=False, skip_group_check=True),
                    reads=("obd", "bfA2"), deps=wdg)
                P.op("vector", lambda v: v.tensor_tensor(
                    bfA2[:, :], bfA1[:, :], bfA1[:, :], op=ALU.mult),
                    reads=("bfA1",), writes=("bfA2",))
                m2 = P.op("tensor", lambda t: t.matmul(
                    gap[:, :], obd[:, :], bfA2[:, :], start=False,
                    stop=True, skip_group_check=True),
                    reads=("obd", "bfA2"))
                P.wend(m2, ("ga",))
                P.op("vector", lambda v: v.tensor_copy(scA1[:, :],
                                                       gap[:, :]),
                    reads=("ga",), writes=("scA1",))
                P.op("scalar", lambda s: s.activation(
                    scA1[:, :], scA1[:, :], AF.Ln,
                    bias=cst[:, CB_EPSR:CB_EPSR + 1], scale=1.0 / 64.0),
                    reads=("scA1", "cst"), writes=("scA1",))
                P.op("scalar", lambda s: s.activation(
                    scA1[:, :], scA1[:, :], AF.Exp, bias=CZ, scale=-0.5),
                    reads=("scA1",), writes=("scA1",))
                P.op("vector", lambda v, g_=g: v.scalar_tensor_tensor(
                    qs2[:, qb0 + g_ * QT:qb0 + (g_ + 1) * QT], bfA1[:, :],
                    qg[:, 0:1], scA1[:, :], op0=ALU.mult, op1=ALU.mult),
                    reads=("bfA1", "qg", "scA1"), writes=(f"{qkey}_{g}",))
                yield
            fns = [(lambda t, c=cc_, st=(cc_ == 0), sp_=(cc_ == NCH - 1):
                    t.matmul(gap[:, :], wkv[:, c * LAT:(c + 1) * LAT],
                             x0bf[:, c * QT:(c + 1) * QT],
                             start=st, stop=sp_, skip_group_check=True),
                    [cast_ops[cc_]])
                   for cc_ in range(NCH)]
            P.group("tensor", fns, writes=("ga",),
                    mreads=[(f"wkv{c}", f"x0bf{c}") for c in range(NCH)])
            P.op("vector", lambda v: v.tensor_copy(kvbf[:, :], gap[:, :]),
                 reads=("ga",), writes=("kvbf",))
            yield
            for g in range(4):
                P.group("tensor",
                        [(lambda t, g_=g: t.matmul(
                            gap[:, :], wk[:, g_ * 128:(g_ + 1) * 128],
                            kvbf[:, :], start=True, stop=True,
                            skip_group_check=True), [])],
                        reads=("wk0", "kvbf"), writes=("ga",))
                P.op("vector", lambda v: v.tensor_copy(bfA1[:, :],
                                                       gap[:, :]),
                    reads=("ga",), writes=("bfA1",))
                P.op("vector", lambda v: v.tensor_tensor(
                    bfA2[:, :], bfA1[:, :], bfA1[:, :], op=ALU.mult),
                    reads=("bfA1",), writes=("bfA2",))
                P.group("tensor",
                        [(lambda t: t.matmul(
                            gbp[:, :], obd[:, :], bfA2[:, :],
                            start=True, stop=True,
                            skip_group_check=True), [])],
                        reads=("obd", "bfA2"), writes=("gb",))
                P.op("vector", lambda v: v.tensor_copy(scA1[:, :],
                                                       gbp[:, :]),
                    reads=("gb",), writes=("scA1",))
                P.op("scalar", lambda s: s.activation(
                    scA1[:, :], scA1[:, :], AF.Ln,
                    bias=cst[:, CB_EPSR:CB_EPSR + 1], scale=1.0 / 32.0),
                    reads=("scA1", "cst"), writes=("scA1",))
                P.op("scalar", lambda s: s.activation(
                    scA1[:, :], scA1[:, :], AF.Exp, bias=CZ, scale=-0.5),
                    reads=("scA1",), writes=("scA1",))
                P.op("vector", lambda v, g_=g, q_=Q: v.scalar_tensor_tensor(
                    ks[:, g_ * T + q_ * QT:g_ * T + (q_ + 1) * QT],
                    bfA1[:, :], kg[:, 0:1], scA1[:, :],
                    op0=ALU.mult, op1=ALU.mult),
                    reads=("bfA1", "kg", "scA1"), writes=(f"ks{g}q{Q}",))
                yield
                kc = 4 * Q + g
                P.group("tensor",
                        [(lambda t, j_=g: t.matmul(
                            gbp[:, :], kvbf[:, j_ * 128:(j_ + 1) * 128],
                            wv[:, :], start=True, stop=True,
                            skip_group_check=True), [])],
                        reads=("kvbf", "wv0"), writes=("gb",))
                P.op("vector", lambda v, k_=kc: v.tensor_copy(
                    bass.AP(vtm.ap().tensor, k_ * 768,
                            [[NKC * 768, 128], [96, 8], [1, 32]]),
                    bass.AP(gbp.ap().tensor, 0,
                            [[512, 128], [64, 8], [1, 32]])),
                    reads=("gb",), writes=(f"vtm{kc}",))
                P.op("vector", lambda v, k_=kc: v.tensor_copy(
                    bass.AP(vtm.ap().tensor, k_ * 768 + 64,
                            [[NKC * 768, 128], [96, 8], [1, 32]]),
                    bass.AP(gbp.ap().tensor, 32,
                            [[512, 128], [64, 8], [1, 32]])),
                    reads=("gb",), writes=(f"vtm{kc}",))
                yield

        def stT(b, Q):
            par = Q % 2
            qb0 = par * 4 * QT
            qkey = f"qs{par}"
            NKCQ = 4 * (Q + 1)
            for g in range(4):
                wdeps_pv = P.wbegin(("pv",))
                wdeps_nm = P.wbegin(("nm",))
                last_pv = last_nm = first_pv = first_nm = None
                pv_groups, nm_groups = [], []
                for kc in range(NKCQ):
                    diag = kc >= 4 * Q
                    jd = kc - 4 * Q
                    qoff = 128 * jd if diag else 0
                    nq = QT - qoff
                    slot = kc % 2
                    ekey = f"eb{slot}"
                    sc_half = []
                    prev_mm = None
                    for hh in range(4):
                        d = [prev_mm]
                        if hh == 0:
                            d += P.wbegin(("sp0", "sp1", "sp2", "sp3"))
                        prev_mm = P.op(
                            "tensor",
                            lambda t, h=hh, g_=g, k_=kc, qo=qoff, n=nq:
                            t.matmul(
                                sp[:, 512 * h + qo:512 * h + qo + n],
                                ks[32 * h:32 * (h + 1),
                                   g_ * T + k_ * KB:
                                   g_ * T + (k_ + 1) * KB],
                                qs2[32 * h:32 * (h + 1),
                                    qb0 + g_ * QT + qo:
                                    qb0 + g_ * QT + qo + n],
                                start=True, stop=True,
                                tile_position=(32 * h, 0),
                                skip_group_check=True),
                            reads=(f"ks{g}q{kc // 4}", f"{qkey}_{g}"),
                            deps=d)
                        if hh == 1:
                            P.wend(prev_mm, ("sp0", "sp1"))
                            sc_half.append(prev_mm)
                        elif hh == 3:
                            P.wend(prev_mm, ("sp2", "sp3"))
                            sc_half.append(prev_mm)
                    e_half = []
                    for half in range(2):
                        e_half.append(P.op(
                            "scalar",
                            lambda s, sl=slot, qo=qoff, n=nq, hf=half:
                            s.activation(
                                ebuf[:, sl * 2048:(sl + 1) * 2048].rearrange(
                                    "p (h q) -> p h q",
                                    h=4)[:, 2 * hf:2 * hf + 2, qo:qo + n],
                                sp[:, :].rearrange(
                                    "p (h q) -> p h q",
                                    h=4)[:, 2 * hf:2 * hf + 2, qo:qo + n],
                                AF.Exp, bias=cst[:, CB_EXPB:CB_EXPB + 1],
                                scale=0.125),
                            reads=(f"sp{2 * half}", f"sp{2 * half + 1}",
                                   "cst"),
                            writes=(f"{ekey}_{half}",)))
                    if diag:
                        m_half = []
                        for half in range(2):
                            m_half.append(P.op(
                                "vector",
                                lambda v, sl=slot, qo=qoff, hf=half:
                                v.tensor_tensor(
                                    ebuf[:, sl * 2048:(sl + 1) * 2048]
                                    .rearrange("p (h q) -> p h q", h=4)
                                    [:, 2 * hf:2 * hf + 2, qo:qo + 128],
                                    ebuf[:, sl * 2048:(sl + 1) * 2048]
                                    .rearrange("p (h q) -> p h q", h=4)
                                    [:, 2 * hf:2 * hf + 2, qo:qo + 128],
                                    tri[:, :].rearrange(
                                        "p (r q) -> p r q", r=2),
                                    op=ALU.mult),
                                reads=(f"{ekey}_{half}", "tri"),
                                writes=(f"{ekey}_{half}",)))
                        e_half = m_half
                    fns = []
                    for jj in range(4):
                        vo = (kc * 768 + 96 * (2 * g + jj // 2)
                              + 32 * (jj % 2))
                        ba_ = pvp if jj < 2 else nmp
                        fns.append((
                            lambda t, j_=jj, vo_=vo, ba=ba_, qo=qoff, n=nq,
                            sl=slot, st=(kc == 0), sp_=(kc == NKCQ - 1):
                            t.matmul(
                                ba[64 * (j_ % 2):64 * (j_ % 2) + 64,
                                   qo:qo + n],
                                vtm[:, vo_:vo_ + 64],
                                ebuf[:, sl * 2048 + 512 * j_ + qo:
                                     sl * 2048 + 512 * j_ + qo + n],
                                start=st, stop=sp_,
                                tile_position=(0, 64 * (j_ % 2)),
                                skip_group_check=True),
                            [e_half[jj // 2]]))
                    g_pv = P.group(
                        "tensor", fns,
                        reads=(f"vtm{kc}", "vones"),
                        mreads=[(f"{ekey}_{jj // 2}",) for jj in range(4)],
                        deps0=((wdeps_pv + wdeps_nm) if kc == 0
                               else [first_pv]))
                    if kc == 0:
                        first_pv = g_pv
                    pv_groups.append(g_pv)
                    last_pv = g_pv
                    yield
                P.wend(last_pv, ("pv", "nm"))
                allg = tuple(pv_groups)
                for sc_, k_, bp_, bk_ in ((scT1, "scT1", pvp, "pv"),
                                          (scT0, "scT0", nmp, "nm")):
                    P.op("scalar", lambda s, d=sc_, p_=bp_: s.activation(
                        d[:, :], p_[:, :], AF.Square, bias=CZ),
                        reads=(bk_, "cst"), writes=(k_,), deps=allg)
                    P.op("scalar", lambda s, d=sc_: s.activation(
                        d[:, :], d[:, :], AF.Ln,
                        bias=cst[:, CB_TINY:CB_TINY + 1]),
                        reads=(k_, "cst"), writes=(k_,))
                    P.op("scalar", lambda s, d=sc_: s.activation(
                        d[:, :], d[:, :], AF.Exp, bias=CZ, scale=-0.5),
                        reads=(k_,), writes=(k_,))
                for jj in range(4):
                    ba_ = pvp if jj < 2 else nmp
                    bk = "pv" if jj < 2 else "nm"
                    sc_, sk = (scT1, "scT1") if jj < 2 else (scT0, "scT0")
                    vr = 64 * (jj % 2) + 32 * (jj % 2)
                    zr = 64 * (jj % 2) + 32 * (1 - jj % 2)
                    P.op("vector", lambda v, j_=jj, ba=ba_, sc2=sc_, g_=g,
                         v_=vr, z_=zr:
                         v.tensor_tensor(
                             outc[32 * j_:32 * j_ + 32,
                                  g_ * QT:(g_ + 1) * QT],
                             ba[v_:v_ + 32, :],
                             sc2[z_:z_ + 32, :],
                             op=ALU.mult),
                         reads=(bk, sk), writes=(f"outc{g}",), deps=allg)
                yield
            yield "wo"
            for dc in range(NCH):
                sl = dc % 2
                load(x1r[:, sl * 512:(sl + 1) * 512],
                     x0_d[b, Q][dc * 128:(dc + 1) * 128, :], f"x1r{sl}",
                     f"x1r{sl}")
                bank, bap = (("pv", pvp) if dc % 2 == 0 else ("nm", nmp))
                fns = [(lambda t, g_=gg, c=dc, ba=bap, st=(gg == 0),
                        sp_=(gg == 3): t.matmul(
                            ba[:, :],
                            wo[:, g_ * D + c * 128:g_ * D + (c + 1) * 128],
                            outc[:, g_ * QT:(g_ + 1) * QT],
                            start=st, stop=sp_, skip_group_check=True), [])
                       for gg in range(4)]
                P.group("tensor", fns, writes=(bank,),
                        mreads=[(f"wo{gg}", f"outc{gg}") for gg in range(4)])
                P.op("vector", lambda v, c=dc, ba=bap, s_=sl:
                     v.scalar_tensor_tensor(
                         ybuf[:, c * (QT + 1) + 1:c * (QT + 1) + 1 + QT],
                         ba[:, :], CZ,
                         x1r[:, s_ * 512:(s_ + 1) * 512],
                         op0=ALU.bypass, op1=ALU.add),
                     reads=(bank, "cst", f"x1r{sl}"), writes=(f"yb{dc}",))
                if dc % 2 == 1:
                    yield
            yield from layer_norm(True, "pv", pvp, "nm", nmp,
                                  scT0, "scT0", scT1, "scT1",
                                  bfT0, "bfT0", bfT1, "bfT1", None, None)

        def layer_norm(inplace, bka, bpa, bkb, bpb, s0, k0, s1, k1,
                       ba, kba, bb, kbb, dst, dkeys):
            wda = P.wbegin((bka,))
            wdb = P.wbegin((bkb,))
            la = lb = None
            for cc in range(NCH):
                base = cc * (QT + 1) + 1
                cba = ba if cc % 2 == 0 else bfT2
                cbk = kba if cc % 2 == 0 else "bfT2"
                P.op("vector", lambda v, b_=base, dd=cba: v.tensor_copy(
                    dd[:, :], ybuf[:, b_:b_ + QT]),
                    reads=(f"yb{cc}",), writes=(cbk,))
                P.op("gpsimd", lambda v, dd=cba: v.tensor_tensor(
                    bb[:, :], dd[:, :], dd[:, :], op=ALU.mult),
                    reads=(cbk,), writes=(kbb,))
                la = P.op("tensor",
                          lambda t, st=(cc == 0), sp_=(cc == NCH - 1),
                          dd=cba:
                          t.matmul(bpa[:, :], onf[:, :], dd[:, :],
                                   start=st, stop=sp_,
                                   skip_group_check=True),
                          reads=("onf", cbk),
                          deps=(wda if cc == 0 else [la]))
                lb = P.op("tensor",
                          lambda t, st=(cc == 0), sp_=(cc == NCH - 1):
                          t.matmul(bpb[:, :], onf[:, :], bb[:, :],
                                   start=st, stop=sp_,
                                   skip_group_check=True),
                          reads=("onf", kbb),
                          deps=(wdb if cc == 0 else [lb]))
            P.wend(la, (bka,))
            P.wend(lb, (bkb,))
            P.op("scalar", lambda s: s.activation(
                s0[:, :], bpa[:, :], AF.Square, bias=CZ, scale=1.0 / 1024.0),
                reads=(bka, "cst"), writes=(k0,))
            P.op("vector", lambda v: v.scalar_tensor_tensor(
                s1[:, :], bpb[:, :], cst[:, CB_R1024:CB_R1024 + 1],
                s0[:, :], op0=ALU.mult, op1=ALU.subtract),
                reads=(bkb, "cst", k0), writes=(k1,))
            P.op("scalar", lambda s: s.activation(
                s1[:, :], s1[:, :], AF.Ln,
                bias=cst[:, CB_EPSL:CB_EPSL + 1]),
                reads=(k1, "cst"), writes=(k1,))
            P.op("scalar", lambda s: s.activation(
                s1[:, :], s1[:, :], AF.Exp, bias=CZ, scale=-0.5),
                reads=(k1,), writes=(k1,))
            P.op("vector", lambda v: v.scalar_tensor_tensor(
                s0[:, :], bpa[:, :], cst[:, CB_RN1024:CB_RN1024 + 1],
                s1[:, :], op0=ALU.mult, op1=ALU.mult),
                reads=(bka, "cst", k1), writes=(k0,))
            yield
            for cc in range(NCH):
                base = cc * (QT + 1) + 1
                if inplace:
                    da, dk = ybuf[:, base:base + QT], f"yb{cc}"
                    P.op("vector", lambda v, b_=base, dd=da: v.tensor_tensor(
                        dd, ybuf[:, b_:b_ + QT], s1[:, :], op=ALU.mult),
                        reads=(dk, k1), writes=(dk,))
                    P.op("vector", lambda v, dd=da: v.tensor_tensor(
                        dd, dd, s0[:, :], op=ALU.add),
                        reads=(dk, k0), writes=(dk,))
                else:
                    da, dk = ybuf[:, base:base + QT], f"yb{cc}"
                    P.op("gpsimd", lambda v, dd=da: v.tensor_tensor(
                        dd, dd, s1[:, :], op=ALU.mult),
                        reads=(dk, k1), writes=(dk,))
                    P.op("gpsimd", lambda v, dd=da: v.tensor_tensor(
                        dd, dd, s0[:, :], op=ALU.add),
                        reads=(dk, k0), writes=(dk,))
                    so = P.op("sync",
                              lambda s, b_=layer_norm.b, q_=layer_norm.q,
                              c=cc, dd=da: s.dma_start(
                                  out_d[b_, q_][c * 128:(c + 1) * 128, :],
                                  dd),
                              reads=(dk,), dma=f"or{cc % 2}")
                    store_ops.append(so)
                if cc % 4 == 3:
                    yield

        def stF(b, Q):
            wrtr = tuple(f"wrt{c}" for c in range(NCH))
            wktr = tuple(f"wkt{c}" for c in range(NCH))
            wvtr = tuple(f"wvt{c}" for c in range(NCH))
            xr_ops, xk_ops, kact_ops = [], [], []
            for cc in range(NCH):
                base = cc * (QT + 1)
                dsb = bfF0 if cc % 2 == 0 else bfF2
                dsk = "bfF0" if cc % 2 == 0 else "bfF2"
                P.op("gpsimd", lambda v, b_=base, dd=dsb: v.tensor_tensor(
                    dd[:, :], ybuf[:, b_ + 1:b_ + 1 + QT],
                    ybuf[:, b_:b_ + QT], op=ALU.subtract),
                    reads=(f"yb{cc}",), writes=(dsk,))
                xr_ops.append(P.op("vector", lambda v, c=cc, b_=base,
                     dd=dsb: v.scalar_tensor_tensor(
                         xr[:, c * QT:(c + 1) * QT], dd[:, :],
                         mur[:, c:c + 1], ybuf[:, b_:b_ + QT],
                         op0=ALU.mult, op1=ALU.add),
                     reads=(dsk, "mur", f"yb{cc}"), writes=(f"xr{cc}",)))
                xk_ops.append(P.op("vector", lambda v, c=cc, b_=base,
                     dd=dsb: v.scalar_tensor_tensor(
                         xk[:, c * QT:(c + 1) * QT], dd[:, :],
                         muk[:, c:c + 1], ybuf[:, b_:b_ + QT],
                         op0=ALU.mult, op1=ALU.add),
                     reads=(dsk, "muk", f"yb{cc}"), writes=(f"xk{cc}",)))
                if cc % 4 == 3:
                    yield
            if Q == n_quarters - 1:
                for cc in range(NCH):
                    P.op("vector", lambda v, c=cc: v.memset(
                        ybuf[:, c * (QT + 1):c * (QT + 1) + 1], 0.0),
                        writes=(f"yb{cc}",))
            else:
                for cc in range(NCH):
                    P.op("vector", lambda v, c=cc: v.tensor_copy(
                        ybuf[:, c * (QT + 1):c * (QT + 1) + 1],
                        ybuf[:, c * (QT + 1) + QT:c * (QT + 1) + QT + 1]),
                        reads=(f"yb{cc}",), writes=(f"yb{cc}",))
            yield
            for dc in range(NCH):
                fns = [(lambda t, c=cc_, d_=dc, st=(cc_ == 0),
                        sp_=(cc_ == NCH - 1): t.matmul(
                            gap[:, :],
                            wrt[:, c * D + d_ * 128:c * D + (d_ + 1) * 128],
                            xr[:, c * QT:(c + 1) * QT],
                            start=st, stop=sp_, skip_group_check=True),
                        [xr_ops[cc_]])
                       for cc_ in range(NCH)]
                P.group("tensor", fns, writes=("ga",),
                        mreads=[(f"wrt{c}", f"xr{c}") for c in range(NCH)])
                P.op("vector", lambda v, c=dc: v.tensor_copy(
                    esig[:, c * QT:(c + 1) * QT], gap[:, :]),
                    reads=("ga",), writes=(f"esig{dc}",))
                if dc % 2 == 1:
                    P.op("scalar", lambda s, c=dc: s.activation(
                        esig[:, (c - 1) * QT:(c + 1) * QT],
                        esig[:, (c - 1) * QT:(c + 1) * QT], AF.Exp,
                        bias=CZ, scale=-1.0),
                        reads=(f"esig{dc - 1}", f"esig{dc}", "cst"),
                        writes=(f"esig{dc - 1}", f"esig{dc}"))
                yield
                fns = [(lambda t, c=cc_, d_=dc, st=(cc_ == 0),
                        sp_=(cc_ == NCH - 1): t.matmul(
                            gbp[:, :],
                            wkt[:, c * D + d_ * 128:c * D + (d_ + 1) * 128],
                            xk[:, c * QT:(c + 1) * QT],
                            start=st, stop=sp_, skip_group_check=True),
                        [xk_ops[cc_]])
                       for cc_ in range(NCH)]
                P.group("tensor", fns, writes=("gb",),
                        mreads=[(f"wkt{c}", f"xk{c}") for c in range(NCH)])
                P.op("vector", lambda v: v.tensor_scalar_max(
                    bfF1[:, :], gbp[:, :], CZ),
                    reads=("gb", "cst"), writes=("bfF1",))
                kact_ops.append(P.op("vector", lambda v, c=dc: v.tensor_tensor(
                    kact[:, c * QT:(c + 1) * QT], bfF1[:, :],
                    bfF1[:, :], op=ALU.mult),
                    reads=("bfF1",), writes=(f"kact{dc}",)))
                yield
            for dc2 in range(NCH):
                bank, bap = (("ga", gap) if dc2 % 2 == 0 else ("gb", gbp))
                fns = [(lambda t, c=cc_, d_=dc2, ba=bap, st=(cc_ == 0),
                        sp_=(cc_ == NCH - 1): t.matmul(
                            ba[:, :],
                            wvt[:, c * D + d_ * 128:c * D + (d_ + 1) * 128],
                            kact[:, c * QT:(c + 1) * QT],
                            start=st, stop=sp_, skip_group_check=True),
                        [kact_ops[cc_]])
                       for cc_ in range(NCH)]
                P.group("tensor", fns, writes=(bank,),
                        mreads=[(f"wvt{c}", f"kact{c}") for c in range(NCH)])
                P.op("vector", lambda v, c=dc2: v.tensor_scalar_add(
                    scF0[:, :], esig[:, c * QT:(c + 1) * QT],
                    cst[:, CB_ONE:CB_ONE + 1]),
                    reads=(f"esig{dc2}", "cst"), writes=("scF0",))
                P.op("scalar", lambda s: s.activation(
                    scF0[:, :], scF0[:, :], AF.Ln, bias=CZ),
                    reads=("scF0", "cst"), writes=("scF0",))
                P.op("scalar", lambda s: s.activation(
                    scF0[:, :], scF0[:, :], AF.Exp, bias=CZ, scale=-1.0),
                    reads=("scF0",), writes=("scF0",))
                P.op("vector", lambda v, ba=bap: v.scalar_tensor_tensor(
                    scF0[:, :], ba[:, :], CZ, scF0[:, :],
                    op0=ALU.bypass, op1=ALU.mult),
                    reads=(bank, "cst", "scF0"), writes=("scF0",))
                P.op("vector", lambda v, c=dc2: v.tensor_tensor(
                    ybuf[:, c * (QT + 1) + 1:c * (QT + 1) + 1 + QT],
                    ybuf[:, c * (QT + 1) + 1:c * (QT + 1) + 1 + QT],
                    scF0[:, :], op=ALU.add),
                    reads=(f"yb{dc2}", "scF0"), writes=(f"yb{dc2}",))
                yield
            layer_norm.b, layer_norm.q = b, Q
            yield from layer_norm(False, "ga", gap, "gb", gbp,
                                  scF0, "scF0", scF1, "scF1",
                                  bfF0, "bfF0", bfF1, "bfF1",
                                  None, None)

        genA = [stA(b, q) for b in range(b_core) for q in range(n_quarters)]
        genT = [stT(b, q) for b in range(b_core) for q in range(n_quarters)]
        genF = [stF(b, q) for b in range(b_core) for q in range(n_quarters)]
        NS = b_core * n_quarters

        def n_units_T(i):
            return 4 * (4 * ((i % n_quarters) + 1) + 1) + 8 + 3
        for s in range(NS + 2):
            drive(genF[s - 2] if 0 <= s - 2 < NS else None,
                  genT[s - 1] if 0 <= s - 1 < NS else None,
                  genA[s] if s < NS else None,
                  nf=34, nt=n_units_T(s - 1) if 1 <= s <= NS else 1, na=12)

        P.op("sync", None, deps=tuple(store_ops))

        dma_c_ops = [o for o in P.all if o.dma == "dma_c"]
        for o in P.all:
            if o.dma is None and any(d.dma == "dma_c" for d in o.deps):
                have = set(map(id, o.deps))
                o.deps = tuple(o.deps) + tuple(
                    x for x in dma_c_ops if id(x) not in have)
        P.schedule()
        P.finalize()
        nc._prog = P
        for o in P.all:
            if o.dma == "dma_c":
                o.count = 16 * 8
        for e in ENGINES:
            for o in P.ops[e]:
                o.waits = [(c, (16 * 8 if c == "dma_c" else v))
                           for (c, v) in o.waits]
        blk = EC(nc.Block())

        def mk(eng_name):
            def emitter(eng):
                for o in P.ops[eng_name]:
                    for (pp, c) in o.waits:
                        eng.wait_ge(sems[pp], c)
                    if o.fn is None:
                        continue
                    instr = o.fn(eng)
                    if o.inc:
                        instr.then_inc(sems[o.chan], 16 if o.dma else 1)
            return emitter

        blk.sync(mk("sync"))
        blk.tensor(mk("tensor"))
        blk.vector(mk("vector"))
        blk.scalar(mk("scalar"))
        blk.gpsimd(mk("gpsimd"))

    return nc



def _prep_inputs(L0_attn_x, L0_attn_Wq, L0_attn_Wkv, L0_attn_Wk, L0_attn_Wv,
                 L0_attn_Wo, L0_attn_q_g, L0_attn_k_g, L0_ffn_W_r,
                 L0_ffn_W_k, L0_ffn_W_v, L0_ffn_mu_r, L0_ffn_mu_k):
    n_quarters = 4
    wq = np.asarray(L0_attn_Wq, dtype=np.float32)
    wq_r = wq.reshape(D, H, HD)
    wq_perm = np.concatenate([wq_r[:, :, :ROPE].reshape(D, H * ROPE),
                              wq_r[:, :, ROPE:].reshape(D, H * ROPE)], axis=1)
    wo = np.asarray(L0_attn_Wo, dtype=np.float32)
    wo_sub = np.ascontiguousarray(
        wo.reshape(H, HD, D)[:, :ROPE, :].reshape(H * ROPE, D))
    shared = {
        "wq": np.ascontiguousarray(wq_perm),
        "wkv": np.ascontiguousarray(np.asarray(L0_attn_Wkv, np.float32)),
        "wk": np.ascontiguousarray(np.asarray(L0_attn_Wk, np.float32)),
        "wv": np.ascontiguousarray(np.asarray(L0_attn_Wv, np.float32)),
        "wo": wo_sub,
        "wrt": np.ascontiguousarray(np.asarray(L0_ffn_W_r, np.float32).T),
        "wkt": np.ascontiguousarray(np.asarray(L0_ffn_W_k, np.float32).T),
        "wvt": np.ascontiguousarray(np.asarray(L0_ffn_W_v, np.float32).T),
        "mur": np.ascontiguousarray(
            np.asarray(L0_ffn_mu_r, np.float32).reshape(NCH, 128).T),
        "muk": np.ascontiguousarray(
            np.asarray(L0_ffn_mu_k, np.float32).reshape(NCH, 128).T),
        "qg": np.ascontiguousarray(
            np.tile(np.asarray(L0_attn_q_g, np.float32)[:ROPE], 4)
            .reshape(128, 1)),
        "kg": np.ascontiguousarray(
            np.tile(np.asarray(L0_attn_k_g, np.float32), 4).reshape(128, 1)),
        "cst": np.tile(np.array([CONST_VALS], np.float32), (128, 1)),
        "tri": np.ascontiguousarray(np.tile(
            (np.arange(128)[None, :] >= np.arange(128)[:, None])
            .astype(bfloat16), (1, 2))),
        "obd": np.ascontiguousarray(
            (np.arange(128)[:, None] // 32
             == np.arange(128)[None, :] // 32).astype(bfloat16)),
        "onf": np.ones((128, 128), dtype=bfloat16),
    }
    x = np.asarray(L0_attn_x, np.float32)
    bsz, t, d = x.shape
    b_core = bsz // N_CORES
    in_maps = []
    for c in range(N_CORES):
        xs = x[c * b_core:(c + 1) * b_core]
        xq = xs.reshape(b_core, n_quarters, QT, d)
        xq = np.ascontiguousarray(xq.transpose(0, 1, 3, 2))
        in_maps.append(dict(shared, x0=xq))
    return in_maps, b_core, n_quarters


def kernel(**inputs):
    in_maps, b_core, n_quarters = _prep_inputs(**inputs)
    nc = build_program(n_quarters=n_quarters, b_core=b_core)
    res = run_bass_kernel_spmd(nc, in_maps, core_ids=list(range(N_CORES)))
    outs = []
    for c in range(N_CORES):
        o = np.asarray(res.results[c]["out"])
        o = o.transpose(0, 1, 3, 2).reshape(b_core, n_quarters * QT, D)
        outs.append(o)
    return np.ascontiguousarray(np.concatenate(outs, axis=0))
